# version 22
# baseline (speedup 1.0000x reference)
import numpy as np
import ml_dtypes
from contextlib import ExitStack

import concourse.bass as bass
import concourse.mybir as mybir
from concourse.bass_utils import run_bass_kernel_spmd

F32 = mybir.dt.float32
BF16 = mybir.dt.bfloat16
AF = mybir.ActivationFunctionType
ALU = mybir.AluOpType

D = 2048
NT = 2048
H = 16
HD = 128
FF = 8192
NJC = D // 128
NTB = NT // 512
NTC = NT // 128
FB = 512
NFB = FF // FB
NFC = FF // 128
ICB = 512
NICB = D // ICB
RSQ_HD = float(1.0 / np.sqrt(HD))

ENG = {"pe": "tensor", "act": "scalar", "dve": "vector", "sp": "sync", "gp": "gpsimd"}
QSEMS = {"gp": ["gpa", "gpb", "gpc"], "sp": ["spa", "spb", "spc"], "st": ["sta", "stb"], "sc": ["sca", "scb"]}
SEMS = ["pe", "act", "dve", "gpe"] + [s for v in QSEMS.values() for s in v]


class Sched:
    def __init__(self):
        self.count = {s: 0 for s in SEMS}
        self.ops = {e: [] for e in ENG}
        self.t_ps = [None] * 8
        self.tc = {}
        self.rot = {q: 0 for q in QSEMS}

    def op(self, eng, fn, waits=(), incs=()):
        flat = []
        for w in waits:
            if w is None:
                continue
            if isinstance(w, list):
                flat.extend(x for x in w if x is not None)
            else:
                flat.append(w)
        waits = flat
        real_incs = []
        toks = []
        for (s, a) in incs:
            if s in QSEMS:
                sems = QSEMS[s]
                sem = sems[self.rot[s] % len(sems)]
                self.rot[s] += 1
                if self.count[sem] > 0:
                    waits.append((sem, self.count[sem]))
                self.count[sem] += a
                real_incs.append((sem, a))
                toks.append((sem, self.count[sem]))
            else:
                self.count[s] += a
                real_incs.append((s, a))
                toks.append((s, self.count[s]))
        self.ops[eng].append((tuple(waits), fn, tuple(real_incs)))
        return toks[0] if len(toks) == 1 else (toks if toks else None)

    def barrier(self):
        return [(s, self.count[s]) for s in SEMS if self.count[s] > 0]

    def emit(self, block, sems):
        own = {"pe": "pe", "act": "act", "dve": "dve", "gp": "gpe"}
        for eng_name, section_name in ENG.items():
            section = getattr(block, section_name)
            ops = self.ops[eng_name]
            own_sem = own.get(eng_name)

            def body(eng, ops=ops, own_sem=own_sem, eng_name=eng_name):
                lastw = {}
                clean = True
                for waits, fn, incs in ops:
                    for (s, v) in waits:
                        if s == own_sem:
                            if eng_name in ("dve", "act", "gp") and not clean:
                                eng.drain()
                                clean = True
                            continue
                        if lastw.get(s, 0) < v:
                            eng.wait_ge(sems[s], v)
                            lastw[s] = v
                    if fn is None:
                        continue
                    inst = fn(eng)
                    clean = False
                    for (s, a) in incs:
                        inst = inst.then_inc(sems[s], a)
            section(body)


def _tile_lhsT(wT, bm=128, bk=128):
    K, M = wT.shape
    return np.ascontiguousarray(
        wT.reshape(K // bk, bk, M // bm, bm).transpose(2, 1, 0, 3).reshape(M // bm, bk, (K // bk) * bm))


def host_prep(inputs):
    f = np.float32
    g = {}
    for nm, key in [("wq1t", "L0_attn_Wq"), ("wk1t", "L0_attn_Wk"),
                    ("wv1t", "L0_attn_Wv"), ("wo1t", "L0_attn_Wo"),
                    ("wq2t", "L0_ffn_Wq"), ("wk2t", "L0_ffn_Wk"),
                    ("wv2t", "L0_ffn_Wv"), ("wo2t", "L0_ffn_Wo")]:
        g[nm] = _tile_lhsT(np.ascontiguousarray(np.asarray(inputs[key], f).T))
    wint = np.ascontiguousarray(np.asarray(inputs["L0_ffn_W_in_base"], f).T)
    wupt = np.ascontiguousarray(np.asarray(inputs["L0_ffn_W_up_base"], f).T)
    g["wint"] = np.ascontiguousarray(wint.reshape(NJC, 128, NFB, FB).transpose(2, 1, 0, 3).reshape(NFB, 128, NJC * FB))
    g["wupt"] = np.ascontiguousarray(wupt.reshape(NJC, 128, NFB, FB).transpose(2, 1, 0, 3).reshape(NFB, 128, NJC * FB))
    wdnt = np.ascontiguousarray(np.asarray(inputs["L0_ffn_W_down"], f).T)
    g["wdnt"] = np.ascontiguousarray(wdnt.reshape(NFC, 128, NICB, ICB).transpose(2, 1, 0, 3).reshape(NICB, 128, NFC * ICB))
    A = np.asarray(inputs["L0_ffn_A"], f)
    g["a_t"] = np.ascontiguousarray(A.reshape(NJC, 128, 16).transpose(1, 0, 2).reshape(128, NJC * 16))
    g["b_t"] = np.ascontiguousarray(np.asarray(inputs["L0_ffn_B"], f))
    g["ln1t"] = np.ascontiguousarray(np.asarray(inputs["L0_ffn_ln1_w"], f).reshape(NJC, 128).T)
    g["ln2t"] = np.ascontiguousarray(np.asarray(inputs["L0_ffn_ln2_w"], f).reshape(NJC, 128).T)

    half = HD // 2
    theta = 10000.0 ** (-np.arange(half, dtype=np.float64) * 2.0 / HD)
    pos = np.arange(NT, dtype=np.float64)
    ang = pos[None, :] * theta[:, None]
    g["ropec"] = np.ascontiguousarray(np.concatenate([np.cos(ang), np.cos(ang)], axis=0).astype(f))
    g["ropes"] = np.ascontiguousarray(np.concatenate([-np.sin(ang), np.sin(ang)], axis=0).astype(f))
    perm = np.zeros((128, 128), f)
    for c in range(128):
        perm[(c + 64) % 128, c] = 1.0
    g["permp"] = perm.astype(ml_dtypes.bfloat16)
    mp = np.zeros((128, 2, 2, 512), f)
    i = np.arange(128)[:, None]
    j = np.arange(512)[None, :]
    for v in range(2):
        for t in range(2):
            mp[:, v, t, :] = ((i + v * 256 + t * 128) <= j).astype(f)
    g["masks"] = np.ascontiguousarray(mp.reshape(128, 2048)).astype(ml_dtypes.bfloat16)
    g["ident"] = np.eye(128, dtype=f).astype(ml_dtypes.bfloat16)

    x = np.asarray(inputs["L0_attn_x"], f)
    xts = [np.ascontiguousarray(x[c].T) for c in range(x.shape[0])]
    return g, xts


def build(dbg=False):
    nc = bass.Bass("TRN2", target_bir_lowering=False)
    S = Sched()
    din = {}

    def dram_in(name, shape, dt=F32):
        din[name] = nc.dram_tensor(name, list(shape), dt, kind="ExternalInput").ap()
        return din[name]

    xT = dram_in("xt", [D, NT])
    for n, shape in [
        ("wq1t", [NJC, 128, NJC * 128]), ("wk1t", [NJC, 128, NJC * 128]),
        ("wv1t", [NJC, 128, NJC * 128]), ("wo1t", [NJC, 128, NJC * 128]),
        ("wq2t", [NJC, 128, NJC * 128]), ("wk2t", [NJC, 128, NJC * 128]),
        ("wv2t", [NJC, 128, NJC * 128]), ("wo2t", [NJC, 128, NJC * 128]),
        ("wint", [NFB, 128, NJC * FB]), ("wupt", [NFB, 128, NJC * FB]),
        ("wdnt", [NICB, 128, NFC * ICB]),
        ("a_t", [128, NJC * 16]), ("b_t", [16, FF]),
        ("ln1t", [128, NJC]), ("ln2t", [128, NJC]),
        ("ropec", [128, NT]), ("ropes", [128, NT]),
    ]:
        dram_in(n, shape)
    dram_in("permp", [128, 128], BF16)
    dram_in("masks", [128, 2048], BF16)
    dram_in("ident", [128, 128], BF16)

    outT = nc.dram_tensor("outt", [D, NT], F32, kind="ExternalOutput").ap()

    skind = "ExternalOutput" if dbg else "Internal"
    scr = {}
    for n, shape, dt in [
        ("xn_sp", [NJC, 128, NT], BF16),
        ("q1_sp", [H, 128, NT], BF16), ("k1_sp", [H, 128, NT], BF16),
        ("v1_sp", [H, NTC, 128, 128], BF16),
        ("q2_sp", [H, 128, NT], BF16), ("k2_sp", [H, 128, NT], BF16),
        ("v2_sp", [H, NTC, 128, 128], BF16),
        ("attno_sp", [NJC, 128, NT], F32),
        ("gu_sp", [NTB, NFC, 128, 512], BF16),
        ("rows_sp", [3, NT], F32),
    ]:
        scr[n] = nc.dram_tensor(n, list(shape), dt, kind=skind).ap()

    with ExitStack() as ctx:
        sbt = lambda name, shape, dt: ctx.enter_context(nc.sbuf_tensor(name, shape, dt))
        PB1 = sbt("pb1", [128, NJC * NT], BF16)
        PB2 = sbt("pb2", [128, NJC * NT], BF16)
        PX = sbt("px", [128, 23552], BF16)
        ROWS = sbt("rows", [1, 6144], F32)
        maskt = sbt("maskt", [128, 2048], BF16)
        permt = sbt("permt", [128, 128], BF16)
        identt = sbt("identt", [128, 128], BF16)
        at = sbt("at", [128, NJC * 16], BF16)
        ln1s = sbt("ln1s", [128, NJC], F32)
        ln2s = sbt("ln2s", [128, NJC], F32)
        ones_bf = sbt("ones_bf", [128, 1], BF16)
        ones128 = sbt("ones128", [128, 128], BF16)
        ones1_f = sbt("ones1_f", [1, 128], F32)
        ones1_b = sbt("ones1_b", [1, 128], BF16)
        bband = sbt("bband", [16, 2 * FB], BF16)
        big = [ctx.enter_context(nc.psum_tensor(f"big{i}", [128, 2048], F32)) for i in range(2)]
        sems = {s: ctx.enter_context(nc.semaphore(f"sem_{s}")) for s in SEMS}
        block = ctx.enter_context(nc.Block())

        PXB = PX[:, :]
        PXF = PX[:, :].bitcast(F32)
        BIGB = [b[:, :].bitcast(BF16) for b in big]

        def pxb(kb, cols):
            return PXB[:, kb * 512: kb * 512 + cols]

        def pxf(kb, cols):
            return PXF[:, kb * 256: kb * 256 + cols]

        def PS(i, rows=128, cols=512):
            return big[i // 4][0:rows, (i % 4) * 512:(i % 4) * 512 + cols]

        R = lambda i: ROWS[0:1, i * 2048:(i + 1) * 2048]

        def set_ps(banks, tok):
            for b in banks:
                S.t_ps[b] = tok

        S.op("sp", lambda e: e.dma_start(out=maskt[:], in_=din["masks"]), incs=[("sp", 16)])
        S.op("sp", lambda e: e.dma_start(out=permt[:], in_=din["permp"]), incs=[("sp", 16)])
        S.op("sp", lambda e: e.dma_start(out=identt[:], in_=din["ident"]), incs=[("sp", 16)])
        S.op("sp", lambda e: e.dma_start(out=ln1s[:], in_=din["ln1t"]), incs=[("sp", 16)])
        S.op("sp", lambda e: e.dma_start(out=ln2s[:], in_=din["ln2t"]), incs=[("sp", 16)])
        t_at = S.op("gp", lambda e: e.dma_start(out=at[:], in_=din["a_t"]), incs=[("gp", 16)])
        S.op("dve", lambda e: e.memset(ones_bf[:], 1.0))
        S.op("dve", lambda e: e.memset(ones128[:], 1.0))
        S.op("dve", lambda e: e.memset(ones1_b[:], 1.0))
        t_ones = S.op("dve", lambda e: e.memset(ones1_f[:], 1.0), incs=[("dve", 1)])

        xbf = [pxb(0, 2048), pxb(4, 2048)]
        xsq = [pxb(8, 2048), pxb(12, 2048)]
        for jc in range(NJC):
            b = jc % 2
            tl = S.op("gp", lambda e, b=b, jc=jc: e.dma_start(out=xbf[b][:], in_=xT[jc * 128:(jc + 1) * 128, :]),
                      waits=[S.tc.get(("xbf", b))], incs=[("gp", 16)])
            ts = S.op("act", lambda e, b=b: e.square(out=xsq[b][:], in_=xbf[b][:]),
                      waits=[tl, S.tc.get(("xsq", b))], incs=[("act", 1)])
            for tb in range(NTB):
                S.op("pe", lambda e, b=b, tb=tb, jc=jc:
                     e.matmul(PS(tb, rows=1), ones_bf[:], xbf[b][:, tb * 512:(tb + 1) * 512],
                              start=(jc == 0), stop=(jc == NJC - 1)),
                     waits=[tl, t_ones] if tb == 0 else [])
            for tb in range(NTB):
                tok = S.op("pe", lambda e, b=b, tb=tb, jc=jc:
                           e.matmul(PS(4 + tb, rows=1), ones_bf[:], xsq[b][:, tb * 512:(tb + 1) * 512],
                                    start=(jc == 0), stop=(jc == NJC - 1)),
                           waits=[ts] if tb == 0 else [],
                           incs=[("pe", 1)] if tb == NTB - 1 else [])
            S.tc[("xbf", b)] = tok
            S.tc[("xsq", b)] = tok
        t_stats = tok

        S.op("dve", lambda e: e.tensor_scalar_mul(out=R(0), in0=big[0][0:1, :], scalar1=1.0 / D),
             waits=[t_stats])
        S.op("dve", lambda e: e.tensor_scalar_mul(out=R(1), in0=big[1][0:1, :], scalar1=1.0 / D))
        S.op("dve", lambda e: e.drain())
        S.op("dve", lambda e: e.tensor_tensor(out=R(2), in0=R(0), in1=R(0), op=ALU.mult))
        S.op("dve", lambda e: e.drain())
        S.op("dve", lambda e: e.tensor_tensor(out=R(1), in0=R(1), in1=R(2), op=ALU.subtract))
        S.op("dve", lambda e: e.drain())
        S.op("dve", lambda e: e.tensor_scalar_add(out=R(2), in0=R(1), scalar1=1e-5))
        S.op("dve", lambda e: e.drain())
        t_u = S.op("dve", lambda e: e.scalar_tensor_tensor(out=R(1), in0=R(2), scalar=1e-6,
                                                           in1=R(1), op0=ALU.mult, op1=ALU.add),
                   incs=[("dve", 1)])
        t_lw = S.op("act", lambda e: e.activation(out=R(2), in_=R(2), func=AF.Ln),
                    waits=[t_u], incs=[("act", 1)])
        t_lu = S.op("act", lambda e: e.activation(out=R(1), in_=R(1), func=AF.Ln),
                    incs=[("act", 1)])
        t_d = S.op("dve", lambda e: e.tensor_tensor(out=R(1), in0=R(2), in1=R(1), op=ALU.subtract),
                   waits=[t_lu], incs=[("dve", 1)])
        t_rshv = S.op("act", lambda e: e.activation(out=R(1), in_=R(1), func=AF.Exp, scale=0.5),
                      waits=[t_d], incs=[("act", 1)])
        t_rsv = S.op("act", lambda e: e.activation(out=R(2), in_=R(2), func=AF.Exp, scale=-0.5),
                     incs=[("act", 1)])
        t_mrs = S.op("dve", lambda e: e.tensor_tensor(out=R(0), in0=R(0), in1=R(2), op=ALU.mult),
                     waits=[t_rsv], incs=[("dve", 1)])
        S.op("sp", lambda e: e.dma_start(out=scr["rows_sp"][0:1, :], in_=R(2)),
             waits=[t_mrs], incs=[("sp", 16)])
        S.op("sp", lambda e: e.dma_start(out=scr["rows_sp"][1:2, :], in_=R(0)),
             waits=[t_mrs], incs=[("sp", 16)])
        S.op("sp", lambda e: e.dma_start(out=scr["rows_sp"][2:3, :], in_=R(1)),
             waits=[t_rshv, t_mrs], incs=[("sp", 16)])
        set_ps(list(range(8)), t_mrs)

        def replicate_row(row_ap, dst_f32, waits, bi=1):
            for tb in range(NTB):
                tm = S.op("pe", lambda e, tb=tb, bi=bi: e.matmul(PS(4 * bi + tb), ones1_f[:],
                          row_ap[:, tb * 512:(tb + 1) * 512], start=True, stop=True),
                          waits=(list(waits) + [t_ones] +
                                 [S.t_ps[b] for b in range(4 * bi, 4 * bi + 4)]) if tb == 0 else [],
                          incs=[("pe", 1)] if tb == NTB - 1 else [])
            te = S.op("dve", lambda e, bi=bi: e.tensor_copy(out=dst_f32[:], in_=big[bi][:, :]),
                      waits=[tm], incs=[("dve", 1)])
            set_ps(list(range(4 * bi, 4 * bi + 4)), te)
            return te

        RSREP = pxf(16, 2048)
        MRSREP = pxf(24, 2048)
        t_rr1 = replicate_row(R(2), RSREP, [t_rsv, t_mrs], bi=0)
        t_rr2 = replicate_row(R(0), MRSREP, [t_mrs], bi=1)

        xq = [pxf(36, 1024), pxf(40, 1024)]
        t1f = pxf(32, 1024)
        t_xn = {}
        for jc in range(NJC):
            tok = None
            for hf in range(2):
                b = (2 * jc + hf) % 2
                tl = S.op("sp", lambda e, b=b, jc=jc, hf=hf:
                          e.dma_start(out=xq[b][:], in_=xT[jc * 128:(jc + 1) * 128, hf * 1024:(hf + 1) * 1024]),
                          waits=[S.tc.get(("xq", b))], incs=[("sp", 16)])
                S.op("dve", lambda e: e.drain())
                S.op("dve", lambda e, b=b, hf=hf: e.tensor_tensor(
                     out=t1f[:], in0=xq[b][:], in1=RSREP[:, hf * 1024:(hf + 1) * 1024], op=ALU.mult),
                     waits=[tl, t_rr1, t_rr2])
                S.op("dve", lambda e: e.drain())
                tok = S.op("dve", lambda e, jc=jc, hf=hf: e.tensor_tensor(
                           out=PB1[:, jc * NT + hf * 1024: jc * NT + (hf + 1) * 1024],
                           in0=t1f[:], in1=MRSREP[:, hf * 1024:(hf + 1) * 1024], op=ALU.subtract),
                           incs=[("dve", 1)])
                S.tc[("xq", b)] = tok
            t_xn[jc] = tok
            S.op("sp", lambda e, jc=jc: e.dma_start(out=scr["xn_sp"][jc], in_=PB1[:, jc * NT:(jc + 1) * NT]),
                 waits=[tok], incs=[("sp", 16)])

        bar_p0 = S.barrier()

        def proj_fmaj(wname, src_pool, out_cb, bar, per_jc=None, wwaits=()):
            wbuf = [pxb(0, 2048), pxb(4, 2048)]
            obuf = [pxb(8, 2048), pxb(12, 2048)]
            tok = None
            for ic in range(NJC):
                b = ic % 2
                bi = ic % 2
                tw = S.op("gp", lambda e, b=b, ic=ic: e.dma_start(out=wbuf[b][:], in_=din[wname][ic]),
                          waits=[S.tc.get(("projw", b))] + (list(wwaits) if ic < 2 else []),
                          incs=[("gp", 16)])
                for jc in range(NJC):
                    for tb in range(NTB):
                        last = (jc == NJC - 1 and tb == NTB - 1)
                        w = []
                        if jc == 0 and tb == 0:
                            w = [tw] + [S.t_ps[k] for k in range(4 * bi, 4 * bi + 4)] + (list(bar) if ic == 0 else [])
                        if per_jc is not None and ic == 0 and tb == 0:
                            w = w + [per_jc.get(jc)]
                        tok = S.op("pe", lambda e, b=b, jc=jc, tb=tb, bi=bi:
                                   e.matmul(PS(4 * bi + tb), wbuf[b][:, jc * 128:(jc + 1) * 128],
                                            src_pool[:, jc * NT + tb * 512: jc * NT + (tb + 1) * 512],
                                            start=(jc == 0), stop=(jc == NJC - 1)),
                                   waits=w, incs=[("pe", 1)] if last else [])
                S.tc[("projw", b)] = tok
                etok = S.op("act", lambda e, b=b, bi=bi: e.activation(out=obuf[b][:], in_=big[bi][:, :],
                                                                      func=AF.Copy),
                            waits=[tok, S.tc.get(("projo", b))], incs=[("act", 1)])
                set_ps(list(range(4 * bi, 4 * bi + 4)), etok)
                S.tc[("projo", b)] = out_cb(ic, etok, obuf[b], b)
            return tok

        ROPEC = pxf(16, 2048)
        ROPES = pxf(24, 2048)

        def rope_and_spill(spill_dst, twaits=()):
            rT = pxf(32, 2048)

            def cb(ic, t_evac, band, b):
                bi = ic % 2
                for tb in range(NTB):
                    trot = S.op("pe", lambda e, tb=tb, bi=bi, band=band:
                                e.matmul(PS(4 * bi + tb), permt[:], band[:, tb * 512:(tb + 1) * 512],
                                         start=True, stop=True),
                                waits=[t_evac] if tb == 0 else [],
                                incs=[("pe", 1)] if tb == NTB - 1 else [])
                t_s = S.op("dve", lambda e, bi=bi: e.tensor_tensor(out=rT[:], in0=big[bi][:, :],
                                                                   in1=ROPES[:], op=ALU.mult),
                           waits=[trot, S.tc.get(("ropert",))] + list(twaits), incs=[("dve", 1)])
                set_ps(list(range(4 * bi, 4 * bi + 4)), t_s)
                S.op("dve", lambda e, band=band: e.tensor_tensor(out=band[:], in0=band[:],
                                                                 in1=ROPEC[:], op=ALU.mult),
                     waits=list(twaits))
                S.op("dve", lambda e: e.drain())
                t_add = S.op("dve", lambda e, band=band: e.tensor_tensor(out=band[:], in0=band[:],
                                                                         in1=rT[:], op=ALU.add),
                             incs=[("dve", 1)])
                S.tc[("ropert",)] = t_add
                t_sp = S.op("sp", lambda e, band=band, ic=ic: e.dma_start(out=spill_dst[ic], in_=band[:]),
                            waits=[t_add], incs=[("sp", 16)])
                return t_sp
            return cb

        def plain_spill(spill_dst):
            def cb(ic, t_evac, band, b):
                return S.op("sp", lambda e, band=band, ic=ic: e.dma_start(out=spill_dst[ic], in_=band[:]),
                            waits=[t_evac], incs=[("sp", 16)])
            return cb

        def vtok_spill(spill_dst, stage_kb=40):
            stage = pxb(stage_kb, 2048)

            def cb(ic, t_evac, band, b):
                bi = ic % 2
                bb = BIGB[bi]
                for tc_ in range(NTC):
                    ttr = S.op("pe", lambda e, tc_=tc_, bb=bb, band=band:
                               e.transpose(bb[:, tc_ * 128:(tc_ + 1) * 128],
                                           band[:, tc_ * 128:(tc_ + 1) * 128], identt[:]),
                               waits=[t_evac] if tc_ == 0 else [],
                               incs=[("pe", 1)] if tc_ == NTC - 1 else [])
                etok = S.op("dve", lambda e, bb=bb: e.tensor_copy(out=stage[:], in_=bb[:, 0:2048]),
                            waits=[ttr, S.tc.get(("vstage",))], incs=[("dve", 1)])
                set_ps(list(range(4 * bi, 4 * bi + 4)), etok)
                stok = S.op("sp", lambda e, ic=ic:
                            e.dma_start(out=spill_dst[ic].rearrange("t p c -> p t c"),
                                        in_=stage[:].rearrange("p (t c) -> p t c", t=NTC)),
                            waits=[etok], incs=[("sp", 16)])
                S.tc[("vstage",)] = stok
                return stok
            return cb

        t_ropec = S.op("sp", lambda e: e.dma_start(out=ROPEC[:], in_=din["ropec"]),
                       waits=[t_rr1, t_rr2], incs=[("sp", 16)])
        t_ropes = S.op("sp", lambda e: e.dma_start(out=ROPES[:], in_=din["ropes"]),
                       waits=[t_rr2], incs=[("sp", 16)])
        px_free = [t_stats]
        proj_fmaj("wq1t", PB1, rope_and_spill(scr["q1_sp"], [t_ropec, t_ropes]),
                  [t_rr1, t_rr2], per_jc=t_xn, wwaits=px_free)
        proj_fmaj("wk1t", PB1, rope_and_spill(scr["k1_sp"], [t_ropec, t_ropes]), [])
        proj_fmaj("wv1t", PB1, vtok_spill(scr["v1_sp"]), [])
        bar_p1a = S.barrier()

        aqh = [pxb(0, 2048), pxb(4, 2048)]
        akh = [pxb(8, 2048), pxb(12, 2048)]
        avh = [pxb(16, 2048), pxb(20, 2048)]
        as4 = [pxb(24, 512), pxb(25, 512), pxb(26, 512), pxb(27, 512)]
        azf = [pxf(28, 512), pxf(30, 512), pxf(32, 512), pxf(34, 512)]
        aqsq = pxb(36, 2048)
        arm1s = [pxf(44, 512), pxf(40, 512)]
        negb16 = pxf(42, 16)
        nmax8 = pxf(43, 8)

        def attention_head(kind, h, sid, q_sp, k_sp, v_sp, outF, bar):
            b = sid
            w0 = list(bar)
            sv_bk = 4 + 2 * sid
            dn_bk = 5 + 2 * sid
            tq = S.op("sp", lambda e, b=b, h=h: e.dma_start(out=aqh[b][:], in_=q_sp[h]),
                      waits=w0 + [S.tc.get(("aq", b))], incs=[("sp", 16)])
            tk = S.op("gp", lambda e, b=b, h=h: e.dma_start(out=akh[b][:], in_=k_sp[h]),
                      waits=w0 + [S.tc.get(("ak", b))], incs=[("gp", 16)])
            tv = S.op("act", lambda e, b=b, h=h:
                      e.dma_start(out=avh[b][:].rearrange("p (t c) -> p t c", t=NTC),
                                  in_=v_sp[h].rearrange("t p c -> p t c")),
                      waits=w0 + [S.tc.get(("av", b))], incs=[("sc", 16)])
            yield

            t_negb = None
            if kind == "sfmx":
                tsq = S.op("dve", lambda e, b=b: e.tensor_tensor(out=aqsq[:], in0=akh[b][:], in1=akh[b][:],
                                                                 op=ALU.mult),
                           waits=[tk, S.tc.get(("aqsq",)), S.tc.get(("aqsqpe",))] + w0,
                           incs=[("dve", 1)])
                for qtr in range(NTB):
                    nbk = sv_bk + qtr % 2
                    tm = S.op("pe", lambda e, qtr=qtr, nbk=nbk: e.matmul(
                              PS(nbk), ones128[:], aqsq[:, qtr * 512:(qtr + 1) * 512],
                              start=True, stop=True),
                              waits=[tsq, S.t_ps[nbk]], incs=[("pe", 1)])
                    tr = S.op("dve", lambda e, qtr=qtr, nbk=nbk: e.reduce_max(
                              out=nmax8[:, qtr:qtr + 1],
                              in_=PS(nbk), axis=mybir.AxisListType.X),
                              waits=[tm], incs=[("dve", 1)])
                    set_ps([nbk], tr)
                S.tc[("nsq",)] = tr
                yield
                tsq2 = S.op("dve", lambda e, b=b: e.tensor_tensor(out=aqsq[:], in0=aqh[b][:], in1=aqh[b][:],
                                                                  op=ALU.mult),
                            waits=[tq, tm], incs=[("dve", 1)])
                for qtr in range(NTB):
                    nbk = sv_bk + qtr % 2
                    tm = S.op("pe", lambda e, qtr=qtr, nbk=nbk: e.matmul(
                              PS(nbk), ones128[:], aqsq[:, qtr * 512:(qtr + 1) * 512],
                              start=True, stop=True),
                              waits=[tsq2, S.t_ps[nbk]], incs=[("pe", 1)])
                    tr = S.op("dve", lambda e, qtr=qtr, nbk=nbk: e.reduce_max(
                              out=nmax8[:, 4 + qtr:4 + qtr + 1],
                              in_=PS(nbk), axis=mybir.AxisListType.X),
                              waits=[tm], incs=[("dve", 1)])
                    set_ps([nbk], tr)
                S.tc[("aqsqpe",)] = tm
                yield
                S.op("dve", lambda e: e.drain())
                S.op("dve", lambda e: e.tensor_tensor(out=nmax8[:, 0:1], in0=nmax8[:, 0:1],
                                                      in1=nmax8[:, 1:2], op=ALU.max))
                S.op("dve", lambda e: e.tensor_tensor(out=nmax8[:, 2:3], in0=nmax8[:, 2:3],
                                                      in1=nmax8[:, 3:4], op=ALU.max))
                S.op("dve", lambda e: e.tensor_tensor(out=nmax8[:, 4:5], in0=nmax8[:, 4:5],
                                                      in1=nmax8[:, 5:6], op=ALU.max))
                S.op("dve", lambda e: e.tensor_tensor(out=nmax8[:, 6:7], in0=nmax8[:, 6:7],
                                                      in1=nmax8[:, 7:8], op=ALU.max))
                S.op("dve", lambda e: e.drain())
                S.op("dve", lambda e: e.tensor_tensor(out=nmax8[:, 0:1], in0=nmax8[:, 0:1],
                                                      in1=nmax8[:, 2:3], op=ALU.max))
                S.op("dve", lambda e: e.tensor_tensor(out=nmax8[:, 4:5], in0=nmax8[:, 4:5],
                                                      in1=nmax8[:, 6:7], op=ALU.max))
                S.op("dve", lambda e: e.drain())
                tpd = S.op("dve", lambda e: e.tensor_tensor(out=nmax8[:, 0:1], in0=nmax8[:, 0:1],
                                                            in1=nmax8[:, 4:5], op=ALU.mult),
                           incs=[("dve", 1)])
                tln = S.op("act", lambda e: e.activation(out=nmax8[:, 0:1], in_=nmax8[:, 0:1], func=AF.Ln),
                           waits=[tpd], incs=[("act", 1)])
                tex = S.op("act", lambda e: e.activation(out=nmax8[:, 0:1], in_=nmax8[:, 0:1],
                                                         func=AF.Exp, scale=0.5),
                           waits=[tln], incs=[("act", 1)])
                t_negb = S.op("dve", lambda e, h=h: e.tensor_scalar_mul(
                              out=negb16[:, h:h + 1], in0=nmax8[:, 0:1], scalar1=-RSQ_HD),
                              waits=[tex], incs=[("dve", 1)])
                yield

            tout = None
            for tqb in range(NTB):
                ntile = 4 * tqb + 4
                t_w = {}
                tsv = None
                for t_ in range(ntile + 2):
                    if t_ >= 2:
                        tt_ = t_ - 2
                        sl2 = 2 * sid + tt_ % 2
                        S.op("pe", lambda e, sl2=sl2, tt_=tt_, b=b, sv_bk=sv_bk,
                             st=(tt_ == 0), en=(tt_ == ntile - 1):
                             e.matmul(PS(sv_bk), avh[b][:, tt_ * 128:(tt_ + 1) * 128],
                                      as4[sl2][:], start=st, stop=en),
                             waits=[t_w[tt_], tv] + ([S.t_ps[sv_bk]] if tt_ == 0 else []))
                        tsv = S.op("pe", lambda e, sl2=sl2, tt_=tt_, dn_bk=dn_bk,
                                   st=(tt_ == 0), en=(tt_ == ntile - 1):
                                   e.matmul(PS(dn_bk), ones128[:], as4[sl2][:], start=st, stop=en),
                                   waits=[S.t_ps[dn_bk]] if tt_ == 0 else [],
                                   incs=[("pe", 1)])
                        S.tc[("as4", sl2)] = tsv
                    if t_ < ntile:
                        sl = 2 * sid + t_ % 2
                        diag = t_ >= 4 * tqb
                        dc = (t_ - 4 * tqb) * 128 if diag else 0
                        tqk = S.op("pe", lambda e, sl=sl, t_=t_, tqb=tqb, b=b:
                                   e.matmul(big[0][:, sl * 512:(sl + 1) * 512],
                                            akh[b][:, t_ * 128:(t_ + 1) * 128],
                                            aqh[b][:, tqb * 512:(tqb + 1) * 512],
                                            start=True, stop=True),
                                   waits=[tq, tk, S.t_ps[sl]], incs=[("pe", 1)])
                        tms = None
                        if dc > 0:
                            tms = S.op("dve", lambda e, sl=sl, dc=dc: e.memset(as4[sl][:, 0:dc], 0.0),
                                       waits=[S.tc.get(("as4", sl))], incs=[("dve", 1)])
                        if kind == "poly":
                            ts2 = S.op("act", lambda e, sl=sl, dc=dc: e.square(
                                       out=azf[sl][:, dc:512],
                                       in_=big[0][:, sl * 512 + dc:(sl + 1) * 512]),
                                       waits=[tqk, S.tc.get(("az", sl))], incs=[("act", 1)])
                            set_ps([sl], ts2)
                            tw4 = S.op("gp", lambda e, sl=sl, dc=dc: e.tensor_tensor(
                                       out=as4[sl][:, dc:512], in0=azf[sl][:, dc:512],
                                       in1=azf[sl][:, dc:512], op=ALU.mult),
                                       waits=[ts2, S.tc.get(("as4", sl))], incs=[("gpe", 1)])
                            S.tc[("az", sl)] = tw4
                        else:
                            tw4 = S.op("act", lambda e, sl=sl, h=h, dc=dc: e.activation(
                                       out=as4[sl][:, dc:512],
                                       in_=big[0][:, sl * 512 + dc:(sl + 1) * 512],
                                       func=AF.Exp, scale=RSQ_HD, bias=negb16[:, h:h + 1]),
                                       waits=[tqk, t_negb, S.tc.get(("as4", sl))],
                                       incs=[("act", 1)])
                            set_ps([sl], tw4)
                            S.tc[("az", sl)] = tw4
                        t_ready = tw4
                        if diag:
                            d = t_ - 4 * tqb
                            bend = min(dc + 128, 512)
                            t_ready = S.op("dve", lambda e, sl=sl, d=d, dc=dc, bend=bend:
                                           e.tensor_tensor(
                                           out=as4[sl][:, dc:bend], in0=as4[sl][:, dc:bend],
                                           in1=maskt[:, d * 512 + dc: d * 512 + bend], op=ALU.mult),
                                           waits=[tw4], incs=[("dve", 1)])
                        t_w[t_] = [t_ready, tms] if tms is not None else t_ready
                    if t_ % 2 == 1:
                        yield
                arm1 = arm1s[sid]
                if kind == "poly":
                    tdm = S.op("dve", lambda e, dn_bk=dn_bk, arm1=arm1: e.tensor_scalar_max(
                               out=arm1[:], in0=PS(dn_bk), scalar1=1e-6),
                               waits=[tsv, S.tc.get(("arm", sid))], incs=[("dve", 1)])
                    set_ps([dn_bk], tdm)
                    tln = S.op("act", lambda e, arm1=arm1: e.activation(out=arm1[:], in_=arm1[:], func=AF.Ln),
                               waits=[tdm, S.tc.get(("arm", sid))], incs=[("act", 1)])
                else:
                    tln = S.op("act", lambda e, dn_bk=dn_bk, arm1=arm1: e.activation(
                               out=arm1[:], in_=PS(dn_bk), func=AF.Ln),
                               waits=[tsv, S.tc.get(("arm", sid))], incs=[("act", 1)])
                    set_ps([dn_bk], tln)
                trc = S.op("act", lambda e, arm1=arm1: e.activation(out=arm1[:], in_=arm1[:], func=AF.Exp,
                                                         scale=-1.0),
                           waits=[tln], incs=[("act", 1)])
                tout = S.op("dve", lambda e, h=h, tqb=tqb, sv_bk=sv_bk, arm1=arm1: e.tensor_tensor(
                            out=outF[:, h * NT + tqb * 512: h * NT + (tqb + 1) * 512],
                            in0=PS(sv_bk), in1=arm1[:], op=ALU.mult),
                            waits=[trc], incs=[("dve", 1)])
                S.tc[("arm", sid)] = tout
                set_ps([sv_bk], tout)
                yield
            S.tc[("aq", b)] = tqk
            S.tc[("ak", b)] = tqk
            S.tc[("av", b)] = tsv

        def proj_out(wname, srcF, band_cb, bar, fixed_bi=None):
            wbuf = [pxb(0, 2048), pxb(4, 2048)]
            obuf = [pxf(8, 2048), pxf(16, 2048)]
            for ic in range(NJC):
                b = ic % 2
                bi = (ic % 2) if fixed_bi is None else fixed_bi
                tw = S.op("gp", lambda e, b=b, ic=ic: e.dma_start(out=wbuf[b][:], in_=din[wname][ic]),
                          waits=[S.tc.get(("projw", b))] + (list(bar) if ic < 2 else []), incs=[("gp", 16)])
                for cin in range(NJC):
                    for tb in range(NTB):
                        last = (cin == NJC - 1 and tb == NTB - 1)
                        tok = S.op("pe", lambda e, b=b, cin=cin, tb=tb, bi=bi:
                                   e.matmul(PS(4 * bi + tb), wbuf[b][:, cin * 128:(cin + 1) * 128],
                                            srcF[:, cin * NT + tb * 512: cin * NT + (tb + 1) * 512],
                                            start=(cin == 0), stop=(cin == NJC - 1)),
                                   waits=([tw] + [S.t_ps[k] for k in range(4 * bi, 4 * bi + 4)] +
                                          (list(bar) if ic == 0 else []))
                                   if (cin == 0 and tb == 0) else [],
                                   incs=[("pe", 1)] if last else [])
                S.tc[("projw", b)] = tok
                etok = S.op("dve", lambda e, b=b, bi=bi: e.tensor_copy(out=obuf[b][:], in_=big[bi][:, :]),
                            waits=[tok, S.tc.get(("projo", b))], incs=[("dve", 1)])
                set_ps(list(range(4 * bi, 4 * bi + 4)), etok)
                S.tc[("projo", b)] = band_cb(ic, etok, obuf[b], b)

        trl = S.op("sp", lambda e: e.dma_start(out=R(2), in_=scr["rows_sp"][2:3, :]),
                   waits=bar_p1a, incs=[("sp", 16)])
        RSHREP = pxf(16, 2048)
        t_rsh = replicate_row(R(2), RSHREP, [trl] + bar_p1a, bi=1)
        h1t = pxf(24, 2048)
        th = None
        for jc in range(NJC):
            tt = S.op("dve", lambda e, jc=jc: e.tensor_tensor(
                      out=h1t[:], in0=PB1[:, jc * NT:(jc + 1) * NT], in1=RSHREP[:], op=ALU.mult),
                      waits=[t_rsh, S.tc.get(("h1t",))], incs=[("dve", 1)])
            th = S.op("act", lambda e, jc=jc: e.activation(
                      out=PB2[:, jc * NT:(jc + 1) * NT], in_=h1t[:], func=AF.Copy,
                      scale=ln1s[:, jc:jc + 1]),
                      waits=[tt], incs=[("act", 1)])
            S.tc[("h1t",)] = th
        bar_h1 = S.barrier()

        proj_fmaj("wq2t", PB2, plain_spill(scr["q2_sp"]), bar_h1)
        proj_fmaj("wk2t", PB2, plain_spill(scr["k2_sp"]), [])
        proj_fmaj("wv2t", PB2, vtok_spill(scr["v2_sp"], stage_kb=32), [])
        bar_p2a = S.barrier()

        for h in range(H):
            bar_h = bar_p2a if h == 0 else []
            ga = attention_head("poly", h, 0, scr["q1_sp"], scr["k1_sp"], scr["v1_sp"], PB1, bar_h)
            gb = attention_head("sfmx", h, 1, scr["q2_sp"], scr["k2_sp"], scr["v2_sp"], PB2, bar_h)
            a_done = b_done = False
            while not (a_done and b_done):
                if not a_done:
                    try:
                        next(ga)
                    except StopIteration:
                        a_done = True
                if not b_done:
                    try:
                        next(gb)
                    except StopIteration:
                        b_done = True
        bar_attn = S.barrier()

        def attno_cb(ic, t_evac, band, b):
            return S.op("sp", lambda e, band=band, ic=ic: e.dma_start(out=scr["attno_sp"][ic], in_=band[:]),
                        waits=[t_evac], incs=[("sp", 16)])
        proj_out("wo1t", PB1, attno_cb, bar_attn)
        bar_p1c = S.barrier()

        TANH1 = pxf(32, 2048)

        def attn2o_cb(ic, t_evac, band, b):
            bb = pxb(24 + 4 * b, 2048)
            tc_ = S.op("dve", lambda e, bb=bb, band=band: e.tensor_copy(out=bb[:], in_=band[:]),
                       waits=[t_evac, S.tc.get(("a2bb", b))], incs=[("dve", 1)])
            tok = None
            for tb in range(NTB):
                tok = S.op("pe", lambda e, tb=tb, ic=ic, bb=bb:
                           e.matmul(PS(4 + tb, rows=16), at[:, ic * 16:(ic + 1) * 16],
                                    bb[:, tb * 512:(tb + 1) * 512],
                                    start=(ic == 0), stop=(ic == NJC - 1)),
                           waits=([tc_, t_at] + ([S.t_ps[k] for k in (4, 5, 6, 7)] if ic == 0 else []))
                           if tb == 0 else [],
                           incs=[("pe", 1)] if tb == NTB - 1 else [])
            S.tc[("a2bb", b)] = tok
            return tok
        proj_out("wo2t", PB2, attn2o_cb, bar_p1c, fixed_bi=0)
        t_moda = S.barrier()
        t_tanh = S.op("act", lambda e: e.activation(out=TANH1[0:16, :], in_=big[1][0:16, :], func=AF.Tanh),
                      waits=t_moda, incs=[("act", 1)])
        set_ps([4, 5, 6, 7], t_tanh)
        bar_p2c = S.barrier()

        trl2 = S.op("sp", lambda e: e.dma_start(out=R(2), in_=scr["rows_sp"][2:3, :]),
                    waits=bar_p2c, incs=[("sp", 16)])
        RSHREP2 = pxf(16, 2048)
        t_rsh2 = replicate_row(R(2), RSHREP2, [trl2] + bar_p2c, bi=1)
        xnb = [pxb(0, 2048), pxb(4, 2048)]
        h2t = pxf(8, 2048)
        t_h2 = {}
        th2 = None
        for jc in range(NJC):
            b = jc % 2
            tl = S.op("sp", lambda e, b=b, jc=jc: e.dma_start(out=xnb[b][:], in_=scr["xn_sp"][jc]),
                      waits=[S.tc.get(("xnb", b))] + (bar_p2c if jc < 2 else []), incs=[("sp", 16)])
            tt = S.op("dve", lambda e, b=b: e.tensor_tensor(out=h2t[:], in0=xnb[b][:], in1=RSHREP2[:],
                                                            op=ALU.mult),
                      waits=[tl, t_rsh2, S.tc.get(("h2t",))], incs=[("dve", 1)])
            th2 = S.op("act", lambda e, jc=jc: e.activation(
                       out=PB1[:, jc * NT:(jc + 1) * NT], in_=h2t[:], func=AF.Copy,
                       scale=ln2s[:, jc:jc + 1]),
                       waits=[tt], incs=[("act", 1)])
            S.tc[("xnb", b)] = th2
            S.tc[("h2t",)] = th2
            t_h2[jc] = th2
        tok = None
        for jc in range(NJC):
            for tb in range(NTB):
                last = (jc == NJC - 1 and tb == NTB - 1)
                tok = S.op("pe", lambda e, jc=jc, tb=tb:
                           e.matmul(PS(tb, rows=16), at[:, jc * 16:(jc + 1) * 16],
                                    PB1[:, jc * NT + tb * 512: jc * NT + (tb + 1) * 512],
                                    start=(jc == 0), stop=(jc == NJC - 1)),
                           waits=([t_h2[jc]] + ([S.t_ps[k] for k in range(4)] if jc == 0 else []))
                           if tb == 0 else [],
                           incs=[("pe", 1)] if last else [])
        t_h2a = tok
        H2A = pxf(24, 2048)
        M2 = pxb(40, 2048)
        te = S.op("dve", lambda e: e.tensor_copy(out=H2A[0:16, :], in_=big[0][0:16, :]),
                  waits=[t_h2a], incs=[("dve", 1)])
        set_ps([0, 1, 2, 3], te)
        S.op("dve", lambda e: e.drain())
        S.op("dve", lambda e: e.tensor_tensor(out=H2A[0:16, :], in0=TANH1[0:16, :], in1=H2A[0:16, :],
                                              op=ALU.mult), waits=[t_tanh])
        S.op("dve", lambda e: e.drain())
        t_m2 = S.op("dve", lambda e: e.tensor_scalar_mul(out=M2[0:16, :], in0=H2A[0:16, :], scalar1=0.1),
                    incs=[("dve", 1)])
        S.op("dve", lambda e: e.drain())

        gub = [pxb(0, 8192), pxb(16, 8192)]
        sgt = pxf(32, 512)
        ui = 0
        tgu = None
        for fb in range(NFB):
            wb = fb % 2
            winb = PB2[:, wb * 8192: wb * 8192 + 8192]
            wupb = PB2[:, 16384 + wb * 8192: 16384 + wb * 8192 + 8192]
            _wprev = S.tc.get(("winb", wb)) or []
            _wprev = _wprev if isinstance(_wprev, list) else [_wprev]
            twi = S.op("gp", lambda e, winb=winb, fb=fb: e.dma_start(out=winb, in_=din["wint"][fb]),
                       waits=_wprev + (bar_p2c if fb < 2 else []), incs=[("gp", 16)])
            twu = S.op("gp", lambda e, wupb=wupb, fb=fb: e.dma_start(out=wupb, in_=din["wupt"][fb]),
                       incs=[("gp", 16)])
            tbb = S.op("gp", lambda e, wb=wb, fb=fb:
                       e.dma_start(out=bband[:, wb * FB:(wb + 1) * FB], in_=din["b_t"][:, fb * FB:(fb + 1) * FB]),
                       incs=[("gp", 16)])
            for fc in range(4):
                for tb in range(NTB):
                    gbk = ui % 4
                    ubk = 4 + ui % 4
                    for jc in range(NJC):
                        S.op("pe", lambda e, jc=jc, fc=fc, tb=tb, gbk=gbk, winb=winb:
                             e.matmul(PS(gbk), winb[:, jc * FB + fc * 128: jc * FB + (fc + 1) * 128],
                                      PB1[:, jc * NT + tb * 512: jc * NT + (tb + 1) * 512],
                                      start=(jc == 0), stop=False),
                             waits=[twi, twu, tbb, t_m2, t_h2[NJC - 1], S.t_ps[gbk]] if jc == 0 else [])
                        S.op("pe", lambda e, jc=jc, fc=fc, tb=tb, ubk=ubk, wupb=wupb:
                             e.matmul(PS(ubk), wupb[:, jc * FB + fc * 128: jc * FB + (fc + 1) * 128],
                                      PB1[:, jc * NT + tb * 512: jc * NT + (tb + 1) * 512],
                                      start=(jc == 0), stop=(jc == NJC - 1)),
                             waits=[S.t_ps[ubk]] if jc == 0 else [])
                    t_gmm = S.op("pe", lambda e, fc=fc, tb=tb, gbk=gbk, wb=wb:
                                 e.matmul(PS(gbk), bband[:, wb * FB + fc * 128: wb * FB + (fc + 1) * 128],
                                          M2[0:16, tb * 512:(tb + 1) * 512], start=False, stop=True),
                                 incs=[("pe", 1)])
                    tsg = S.op("act", lambda e, gbk=gbk: e.activation(out=sgt[:], in_=PS(gbk), func=AF.Silu),
                               waits=[t_gmm, S.tc.get(("sgt",))], incs=[("act", 1)])
                    _gprev = S.tc.get(("gub", wb)) or []
                    _gprev = _gprev if isinstance(_gprev, list) else [_gprev]
                    tgu = S.op("dve", lambda e, fc=fc, tb=tb, ubk=ubk, wb=wb:
                               e.tensor_tensor(out=gub[wb][:, fc * 2048 + tb * 512: fc * 2048 + (tb + 1) * 512],
                                               in0=sgt[:], in1=PS(ubk), op=ALU.mult),
                               waits=[tsg] + _gprev, incs=[("dve", 1)])
                    S.tc[("sgt",)] = tgu
                    set_ps([gbk, ubk], tgu)
                    ui += 1
            tspills = []
            for tbx in range(NTB):
                tgsp = S.op("sp", lambda e, wb=wb, fb=fb, tbx=tbx:
                            e.dma_start(out=scr["gu_sp"][tbx, fb * 4:(fb + 1) * 4].rearrange("f p t -> p f t"),
                                        in_=gub[wb].rearrange("p (f r) -> p f r", f=4)[:, :, tbx * 512:(tbx + 1) * 512]),
                            waits=[tgu] if tbx == 0 else [], incs=[("sp", 16)])
                tspills.append(tgsp)
            S.tc[("winb", wb)] = tspills
            S.tc[("gub", wb)] = tspills
        bar_p3a = S.barrier()

        trs = S.op("sp", lambda e: e.dma_start(out=R(2), in_=scr["rows_sp"][0:1, :]),
                   waits=bar_p3a, incs=[("sp", 16)])
        trm = S.op("sp", lambda e: e.dma_start(out=R(1), in_=scr["rows_sp"][1:2, :]),
                   incs=[("sp", 16)])
        ta1 = S.op("dve", lambda e: e.tensor_scalar_add(out=R(2), in0=R(2), scalar1=1.0),
                   waits=[trs] + bar_p3a, incs=[("dve", 1)])
        AREP = pxf(16, 2048)
        MREP = pxf(24, 2048)
        t_ar = replicate_row(R(2), AREP, [ta1] + bar_p3a, bi=1)
        t_mr = replicate_row(R(1), MREP, [trm], bi=1)

        xtile = [pxf(32, 512), pxf(34, 512)]
        aotile = [pxf(36, 512), pxf(38, 512)]
        e1 = pxf(40, 512)
        e2 = pxf(42, 512)
        for icb in range(NICB):
            twd = {}
            for s in range(2):
                twd[s] = S.op("gp", lambda e, icb=icb, s=s: e.dma_start(
                              out=PB2[:, s * 16384:(s + 1) * 16384],
                              in_=din["wdnt"][icb][:, s * 16384:(s + 1) * 16384]),
                              waits=[S.tc.get(("wdh", s))] + (bar_p3a if icb == 0 else []),
                              incs=[("gp", 16)])
            for tqb in range(NTB):
                tgul = {}
                for s in range(2):
                    tgul[s] = S.op("gp", lambda e, tqb=tqb, s=s: e.dma_start(
                                   out=PB1[:, s * 16384:(s + 1) * 16384].rearrange("p (f t) -> p f t", f=32),
                                   in_=scr["gu_sp"][tqb, 32 * s:32 * s + 32].rearrange("f p t -> p f t")),
                                   waits=[S.tc.get(("guh", s))] + (bar_p3a if icb == 0 and tqb == 0 else []),
                                   incs=[("gp", 16)])
                tok = None
                t_mlps = {}
                for s in range(2):
                    for ici in range(4):
                        bk = ici % 4
                        for fc in range(32 * s, 32 * s + 32):
                            last = (fc == NFC - 1)
                            tok = S.op("pe", lambda e, fc=fc, ici=ici, bk=bk, last=last:
                                       e.matmul(PS(bk), PB2[:, fc * ICB + ici * 128: fc * ICB + (ici + 1) * 128],
                                                PB1[:, fc * 512:(fc + 1) * 512], start=(fc == 0), stop=last),
                                       waits=([twd[s], tgul[s]] + ([S.t_ps[bk]] if s == 0 else []))
                                       if fc == 32 * s else [],
                                       incs=[("pe", 1)] if (last or fc == 32 * s + 31) else [])
                        if s == 1:
                            t_mlps[ici] = tok
                    S.tc[("guh", s)] = tok
                    S.tc[("wdh", s)] = tok
                for ici in range(4):
                    ic = icb * 4 + ici
                    bk = ici % 4
                    t_mlp = t_mlps[ici]
                    eb = ici % 2
                    tlx = S.op("act", lambda e, ic=ic, tqb=tqb, eb=eb:
                               e.dma_start(out=xtile[eb][:],
                                           in_=xT[ic * 128:(ic + 1) * 128, tqb * 512:(tqb + 1) * 512]),
                               waits=[S.tc.get(("xts", eb))] + (bar_p3a if icb == 0 and tqb == 0 else []),
                               incs=[("sc", 16)])
                    tla = S.op("act", lambda e, ic=ic, tqb=tqb, eb=eb:
                               e.dma_start(out=aotile[eb][:],
                                           in_=scr["attno_sp"][ic][:, tqb * 512:(tqb + 1) * 512]),
                               incs=[("sc", 16)])
                    te1 = S.op("dve", lambda e, eb=eb, tqb=tqb: e.tensor_tensor(
                               out=e1[:], in0=xtile[eb][:], in1=AREP[:, tqb * 512:(tqb + 1) * 512],
                               op=ALU.mult),
                               waits=[tlx, t_ar, t_mr, S.tc.get(("e1",))], incs=[("dve", 1)])
                    S.op("dve", lambda e: e.drain())
                    S.op("dve", lambda e, tqb=tqb: e.tensor_tensor(
                         out=e1[:], in0=e1[:], in1=MREP[:, tqb * 512:(tqb + 1) * 512], op=ALU.subtract))
                    te2 = S.op("dve", lambda e, eb=eb, bk=bk: e.tensor_tensor(
                               out=e2[:], in0=PS(bk), in1=aotile[eb][:], op=ALU.add),
                               waits=[t_mlp, tla, S.tc.get(("e2",))], incs=[("dve", 1)])
                    set_ps([bk], te2)
                    S.op("dve", lambda e: e.drain())
                    teo = S.op("dve", lambda e: e.tensor_tensor(out=e1[:], in0=e1[:], in1=e2[:], op=ALU.add),
                               incs=[("dve", 1)])
                    S.op("dve", lambda e: e.drain())
                    S.tc[("e2",)] = teo
                    tst = S.op("sp", lambda e, ic=ic, tqb=tqb: e.dma_start(
                               out=outT[ic * 128:(ic + 1) * 128, tqb * 512:(tqb + 1) * 512], in_=e1[:]),
                               waits=[teo], incs=[("st", 16)])
                    S.tc[("e1",)] = tst
                    S.tc[("xts", eb)] = tst

        S.op("sp", None, waits=[(s, S.count[s]) for s in QSEMS["st"] if S.count[s] > 0])
        S.emit(block, sems)
    return nc


def kernel(**inputs):
    g, xts = host_prep(inputs)
    nc = build()
    in_maps = []
    for c in range(8):
        m = dict(g)
        m["xt"] = xts[c]
        in_maps.append(m)
    res = run_bass_kernel_spmd(nc, in_maps, core_ids=list(range(8)))
    outs = [np.ascontiguousarray(np.asarray(r["outt"], np.float32).T) for r in res.results]
    return np.stack(outs)


# version 23
# speedup vs baseline: 1.0110x; 1.0110x over previous
import numpy as np
import ml_dtypes
from contextlib import ExitStack

import concourse.bass as bass
import concourse.mybir as mybir
from concourse.bass_utils import run_bass_kernel_spmd

F32 = mybir.dt.float32
BF16 = mybir.dt.bfloat16
AF = mybir.ActivationFunctionType
ALU = mybir.AluOpType

D = 2048
NT = 2048
H = 16
HD = 128
FF = 8192
NJC = D // 128
NTB = NT // 512
NTC = NT // 128
FB = 512
NFB = FF // FB
NFC = FF // 128
ICB = 512
NICB = D // ICB
RSQ_HD = float(1.0 / np.sqrt(HD))

ENG = {"pe": "tensor", "act": "scalar", "dve": "vector", "sp": "sync", "gp": "gpsimd"}
QSEMS = {"gp": ["gpa", "gpb", "gpc"], "sp": ["spa", "spb", "spc"], "st": ["sta", "stb"], "sc": ["sca", "scb"]}
SEMS = ["pe", "act", "dve", "gpe"] + [s for v in QSEMS.values() for s in v]


class Sched:
    def __init__(self):
        self.count = {s: 0 for s in SEMS}
        self.ops = {e: [] for e in ENG}
        self.t_ps = [None] * 8
        self.tc = {}
        self.rot = {q: 0 for q in QSEMS}

    def op(self, eng, fn, waits=(), incs=()):
        flat = []
        for w in waits:
            if w is None:
                continue
            if isinstance(w, list):
                flat.extend(x for x in w if x is not None)
            else:
                flat.append(w)
        waits = flat
        real_incs = []
        toks = []
        for (s, a) in incs:
            if s in QSEMS:
                sems = QSEMS[s]
                sem = sems[self.rot[s] % len(sems)]
                self.rot[s] += 1
                if self.count[sem] > 0:
                    waits.append((sem, self.count[sem]))
                self.count[sem] += a
                real_incs.append((sem, a))
                toks.append((sem, self.count[sem]))
            else:
                self.count[s] += a
                real_incs.append((s, a))
                toks.append((s, self.count[s]))
        self.ops[eng].append((tuple(waits), fn, tuple(real_incs)))
        return toks[0] if len(toks) == 1 else (toks if toks else None)

    def barrier(self):
        return [(s, self.count[s]) for s in SEMS if self.count[s] > 0]

    def emit(self, block, sems):
        own = {"pe": "pe", "act": "act", "dve": "dve", "gp": "gpe"}
        for eng_name, section_name in ENG.items():
            section = getattr(block, section_name)
            ops = self.ops[eng_name]
            own_sem = own.get(eng_name)

            def body(eng, ops=ops, own_sem=own_sem, eng_name=eng_name):
                lastw = {}
                clean = True
                for waits, fn, incs in ops:
                    for (s, v) in waits:
                        if s == own_sem:
                            if eng_name in ("dve", "act", "gp") and not clean:
                                eng.drain()
                                clean = True
                            continue
                        if lastw.get(s, 0) < v:
                            eng.wait_ge(sems[s], v)
                            lastw[s] = v
                    if fn is None:
                        continue
                    inst = fn(eng)
                    clean = False
                    for (s, a) in incs:
                        inst = inst.then_inc(sems[s], a)
            section(body)


def _tile_lhsT(wT, bm=128, bk=128):
    K, M = wT.shape
    return np.ascontiguousarray(
        wT.reshape(K // bk, bk, M // bm, bm).transpose(2, 1, 0, 3).reshape(M // bm, bk, (K // bk) * bm))


def host_prep(inputs):
    f = np.float32
    g = {}
    for nm, key in [("wq1t", "L0_attn_Wq"), ("wk1t", "L0_attn_Wk"),
                    ("wv1t", "L0_attn_Wv"), ("wo1t", "L0_attn_Wo"),
                    ("wq2t", "L0_ffn_Wq"), ("wk2t", "L0_ffn_Wk"),
                    ("wv2t", "L0_ffn_Wv"), ("wo2t", "L0_ffn_Wo")]:
        g[nm] = _tile_lhsT(np.ascontiguousarray(np.asarray(inputs[key], f).T))
    wint = np.ascontiguousarray(np.asarray(inputs["L0_ffn_W_in_base"], f).T)
    wupt = np.ascontiguousarray(np.asarray(inputs["L0_ffn_W_up_base"], f).T)
    g["wint"] = np.ascontiguousarray(wint.reshape(NJC, 128, NFB, FB).transpose(2, 1, 0, 3).reshape(NFB, 128, NJC * FB))
    g["wupt"] = np.ascontiguousarray(wupt.reshape(NJC, 128, NFB, FB).transpose(2, 1, 0, 3).reshape(NFB, 128, NJC * FB))
    wdnt = np.ascontiguousarray(np.asarray(inputs["L0_ffn_W_down"], f).T)
    g["wdnt"] = np.ascontiguousarray(wdnt.reshape(NFC, 128, NICB, ICB).transpose(2, 1, 0, 3).reshape(NICB, 128, NFC * ICB))
    A = np.asarray(inputs["L0_ffn_A"], f)
    g["a_t"] = np.ascontiguousarray(A.reshape(NJC, 128, 16).transpose(1, 0, 2).reshape(128, NJC * 16))
    g["b_t"] = np.ascontiguousarray(np.asarray(inputs["L0_ffn_B"], f))
    g["ln1t"] = np.ascontiguousarray(np.asarray(inputs["L0_ffn_ln1_w"], f).reshape(NJC, 128).T)
    g["ln2t"] = np.ascontiguousarray(np.asarray(inputs["L0_ffn_ln2_w"], f).reshape(NJC, 128).T)

    half = HD // 2
    theta = 10000.0 ** (-np.arange(half, dtype=np.float64) * 2.0 / HD)
    pos = np.arange(NT, dtype=np.float64)
    ang = pos[None, :] * theta[:, None]
    g["ropec"] = np.ascontiguousarray(np.concatenate([np.cos(ang), np.cos(ang)], axis=0).astype(f))
    g["ropes"] = np.ascontiguousarray(np.concatenate([-np.sin(ang), np.sin(ang)], axis=0).astype(f))
    perm = np.zeros((128, 128), f)
    for c in range(128):
        perm[(c + 64) % 128, c] = 1.0
    g["permp"] = perm.astype(ml_dtypes.bfloat16)
    mp = np.zeros((128, 2, 2, 512), f)
    i = np.arange(128)[:, None]
    j = np.arange(512)[None, :]
    for v in range(2):
        for t in range(2):
            mp[:, v, t, :] = ((i + v * 256 + t * 128) <= j).astype(f)
    g["masks"] = np.ascontiguousarray(mp.reshape(128, 2048)).astype(ml_dtypes.bfloat16)
    g["ident"] = np.eye(128, dtype=f).astype(ml_dtypes.bfloat16)

    x = np.asarray(inputs["L0_attn_x"], f)
    xts = [np.ascontiguousarray(x[c].T) for c in range(x.shape[0])]
    return g, xts


def build(dbg=False):
    nc = bass.Bass("TRN2", target_bir_lowering=False)
    S = Sched()
    din = {}

    def dram_in(name, shape, dt=F32):
        din[name] = nc.dram_tensor(name, list(shape), dt, kind="ExternalInput").ap()
        return din[name]

    xT = dram_in("xt", [D, NT])
    for n, shape in [
        ("wq1t", [NJC, 128, NJC * 128]), ("wk1t", [NJC, 128, NJC * 128]),
        ("wv1t", [NJC, 128, NJC * 128]), ("wo1t", [NJC, 128, NJC * 128]),
        ("wq2t", [NJC, 128, NJC * 128]), ("wk2t", [NJC, 128, NJC * 128]),
        ("wv2t", [NJC, 128, NJC * 128]), ("wo2t", [NJC, 128, NJC * 128]),
        ("wint", [NFB, 128, NJC * FB]), ("wupt", [NFB, 128, NJC * FB]),
        ("wdnt", [NICB, 128, NFC * ICB]),
        ("a_t", [128, NJC * 16]), ("b_t", [16, FF]),
        ("ln1t", [128, NJC]), ("ln2t", [128, NJC]),
        ("ropec", [128, NT]), ("ropes", [128, NT]),
    ]:
        dram_in(n, shape)
    dram_in("permp", [128, 128], BF16)
    dram_in("masks", [128, 2048], BF16)
    dram_in("ident", [128, 128], BF16)

    outT = nc.dram_tensor("outt", [D, NT], F32, kind="ExternalOutput").ap()

    skind = "ExternalOutput" if dbg else "Internal"
    scr = {}
    for n, shape, dt in [
        ("xn_sp", [NJC, 128, NT], BF16),
        ("q1_sp", [H, 128, NT], BF16), ("k1_sp", [H, 128, NT], BF16),
        ("v1_sp", [H, NTC, 128, 128], BF16),
        ("q2_sp", [H, 128, NT], BF16), ("k2_sp", [H, 128, NT], BF16),
        ("v2_sp", [H, NTC, 128, 128], BF16),
        ("attno_sp", [NJC, 128, NT], F32),
        ("gu_sp", [NTB, NFC, 128, 512], BF16),
        ("rows_sp", [3, NT], F32),
    ]:
        scr[n] = nc.dram_tensor(n, list(shape), dt, kind=skind).ap()

    with ExitStack() as ctx:
        sbt = lambda name, shape, dt: ctx.enter_context(nc.sbuf_tensor(name, shape, dt))
        PB1 = sbt("pb1", [128, NJC * NT], BF16)
        PB2 = sbt("pb2", [128, NJC * NT], BF16)
        PX = sbt("px", [128, 23552], BF16)
        ROWS = sbt("rows", [1, 6144], F32)
        maskt = sbt("maskt", [128, 2048], BF16)
        permt = sbt("permt", [128, 128], BF16)
        identt = sbt("identt", [128, 128], BF16)
        at = sbt("at", [128, NJC * 16], BF16)
        ln1s = sbt("ln1s", [128, NJC], F32)
        ln2s = sbt("ln2s", [128, NJC], F32)
        ones_bf = sbt("ones_bf", [128, 1], BF16)
        ones128 = sbt("ones128", [128, 128], BF16)
        ones1_f = sbt("ones1_f", [1, 128], F32)
        ones1_b = sbt("ones1_b", [1, 128], BF16)
        bband = sbt("bband", [16, 2 * FB], BF16)
        big = [ctx.enter_context(nc.psum_tensor(f"big{i}", [128, 2048], F32)) for i in range(2)]
        sems = {s: ctx.enter_context(nc.semaphore(f"sem_{s}")) for s in SEMS}
        block = ctx.enter_context(nc.Block())

        PXB = PX[:, :]
        PXF = PX[:, :].bitcast(F32)
        BIGB = [b[:, :].bitcast(BF16) for b in big]

        def pxb(kb, cols):
            return PXB[:, kb * 512: kb * 512 + cols]

        def pxf(kb, cols):
            return PXF[:, kb * 256: kb * 256 + cols]

        def PS(i, rows=128, cols=512):
            return big[i // 4][0:rows, (i % 4) * 512:(i % 4) * 512 + cols]

        R = lambda i: ROWS[0:1, i * 2048:(i + 1) * 2048]

        def set_ps(banks, tok):
            for b in banks:
                S.t_ps[b] = tok

        S.op("sp", lambda e: e.dma_start(out=maskt[:], in_=din["masks"]), incs=[("sp", 16)])
        S.op("sp", lambda e: e.dma_start(out=permt[:], in_=din["permp"]), incs=[("sp", 16)])
        S.op("sp", lambda e: e.dma_start(out=identt[:], in_=din["ident"]), incs=[("sp", 16)])
        S.op("sp", lambda e: e.dma_start(out=ln1s[:], in_=din["ln1t"]), incs=[("sp", 16)])
        S.op("sp", lambda e: e.dma_start(out=ln2s[:], in_=din["ln2t"]), incs=[("sp", 16)])
        t_at = S.op("gp", lambda e: e.dma_start(out=at[:], in_=din["a_t"]), incs=[("gp", 16)])
        S.op("dve", lambda e: e.memset(ones_bf[:], 1.0))
        S.op("dve", lambda e: e.memset(ones128[:], 1.0))
        S.op("dve", lambda e: e.memset(ones1_b[:], 1.0))
        t_ones = S.op("dve", lambda e: e.memset(ones1_f[:], 1.0), incs=[("dve", 1)])

        xbf = [pxb(0, 2048), pxb(4, 2048)]
        xsq = [pxb(8, 2048), pxb(12, 2048)]
        for jc in range(NJC):
            b = jc % 2
            tl = S.op("gp", lambda e, b=b, jc=jc: e.dma_start(out=xbf[b][:], in_=xT[jc * 128:(jc + 1) * 128, :]),
                      waits=[S.tc.get(("xbf", b))], incs=[("gp", 16)])
            ts = S.op("act", lambda e, b=b: e.square(out=xsq[b][:], in_=xbf[b][:]),
                      waits=[tl, S.tc.get(("xsq", b))], incs=[("act", 1)])
            for tb in range(NTB):
                S.op("pe", lambda e, b=b, tb=tb, jc=jc:
                     e.matmul(PS(tb, rows=1), ones_bf[:], xbf[b][:, tb * 512:(tb + 1) * 512],
                              start=(jc == 0), stop=(jc == NJC - 1)),
                     waits=[tl, t_ones] if tb == 0 else [])
            for tb in range(NTB):
                tok = S.op("pe", lambda e, b=b, tb=tb, jc=jc:
                           e.matmul(PS(4 + tb, rows=1), ones_bf[:], xsq[b][:, tb * 512:(tb + 1) * 512],
                                    start=(jc == 0), stop=(jc == NJC - 1)),
                           waits=[ts] if tb == 0 else [],
                           incs=[("pe", 1)] if tb == NTB - 1 else [])
            S.tc[("xbf", b)] = tok
            S.tc[("xsq", b)] = tok
        t_stats = tok

        S.op("dve", lambda e: e.tensor_scalar_mul(out=R(0), in0=big[0][0:1, :], scalar1=1.0 / D),
             waits=[t_stats])
        S.op("dve", lambda e: e.tensor_scalar_mul(out=R(1), in0=big[1][0:1, :], scalar1=1.0 / D))
        S.op("dve", lambda e: e.drain())
        S.op("dve", lambda e: e.tensor_tensor(out=R(2), in0=R(0), in1=R(0), op=ALU.mult))
        S.op("dve", lambda e: e.drain())
        S.op("dve", lambda e: e.tensor_tensor(out=R(1), in0=R(1), in1=R(2), op=ALU.subtract))
        S.op("dve", lambda e: e.drain())
        S.op("dve", lambda e: e.tensor_scalar_add(out=R(2), in0=R(1), scalar1=1e-5))
        S.op("dve", lambda e: e.drain())
        t_u = S.op("dve", lambda e: e.scalar_tensor_tensor(out=R(1), in0=R(2), scalar=1e-6,
                                                           in1=R(1), op0=ALU.mult, op1=ALU.add),
                   incs=[("dve", 1)])
        t_lw = S.op("act", lambda e: e.activation(out=R(2), in_=R(2), func=AF.Ln),
                    waits=[t_u], incs=[("act", 1)])
        t_lu = S.op("act", lambda e: e.activation(out=R(1), in_=R(1), func=AF.Ln),
                    incs=[("act", 1)])
        t_d = S.op("dve", lambda e: e.tensor_tensor(out=R(1), in0=R(2), in1=R(1), op=ALU.subtract),
                   waits=[t_lu], incs=[("dve", 1)])
        t_rshv = S.op("act", lambda e: e.activation(out=R(1), in_=R(1), func=AF.Exp, scale=0.5),
                      waits=[t_d], incs=[("act", 1)])
        t_rsv = S.op("act", lambda e: e.activation(out=R(2), in_=R(2), func=AF.Exp, scale=-0.5),
                     incs=[("act", 1)])
        t_mrs = S.op("dve", lambda e: e.tensor_tensor(out=R(0), in0=R(0), in1=R(2), op=ALU.mult),
                     waits=[t_rsv], incs=[("dve", 1)])
        S.op("sp", lambda e: e.dma_start(out=scr["rows_sp"][0:1, :], in_=R(2)),
             waits=[t_mrs], incs=[("sp", 16)])
        S.op("sp", lambda e: e.dma_start(out=scr["rows_sp"][1:2, :], in_=R(0)),
             waits=[t_mrs], incs=[("sp", 16)])
        S.op("sp", lambda e: e.dma_start(out=scr["rows_sp"][2:3, :], in_=R(1)),
             waits=[t_rshv, t_mrs], incs=[("sp", 16)])
        set_ps(list(range(8)), t_mrs)

        def replicate_row(row_ap, dst_f32, waits, bi=1):
            for tb in range(NTB):
                tm = S.op("pe", lambda e, tb=tb, bi=bi: e.matmul(PS(4 * bi + tb), ones1_f[:],
                          row_ap[:, tb * 512:(tb + 1) * 512], start=True, stop=True),
                          waits=(list(waits) + [t_ones] +
                                 [S.t_ps[b] for b in range(4 * bi, 4 * bi + 4)]) if tb == 0 else [],
                          incs=[("pe", 1)] if tb == NTB - 1 else [])
            te = S.op("dve", lambda e, bi=bi: e.tensor_copy(out=dst_f32[:], in_=big[bi][:, :]),
                      waits=[tm], incs=[("dve", 1)])
            set_ps(list(range(4 * bi, 4 * bi + 4)), te)
            return te

        RSREP = pxf(16, 2048)
        MRSREP = pxf(24, 2048)
        t_rr1 = replicate_row(R(2), RSREP, [t_rsv, t_mrs], bi=0)
        t_rr2 = replicate_row(R(0), MRSREP, [t_mrs], bi=1)

        xq = [pxf(36, 1024), pxf(40, 1024)]
        t1f = pxf(32, 1024)
        t_xn = {}
        for jc in range(NJC):
            tok = None
            for hf in range(2):
                b = (2 * jc + hf) % 2
                tl = S.op("sp", lambda e, b=b, jc=jc, hf=hf:
                          e.dma_start(out=xq[b][:], in_=xT[jc * 128:(jc + 1) * 128, hf * 1024:(hf + 1) * 1024]),
                          waits=[S.tc.get(("xq", b))], incs=[("sp", 16)])
                S.op("dve", lambda e: e.drain())
                S.op("dve", lambda e, b=b, hf=hf: e.tensor_tensor(
                     out=t1f[:], in0=xq[b][:], in1=RSREP[:, hf * 1024:(hf + 1) * 1024], op=ALU.mult),
                     waits=[tl, t_rr1, t_rr2])
                S.op("dve", lambda e: e.drain())
                tok = S.op("dve", lambda e, jc=jc, hf=hf: e.tensor_tensor(
                           out=PB1[:, jc * NT + hf * 1024: jc * NT + (hf + 1) * 1024],
                           in0=t1f[:], in1=MRSREP[:, hf * 1024:(hf + 1) * 1024], op=ALU.subtract),
                           incs=[("dve", 1)])
                S.tc[("xq", b)] = tok
            t_xn[jc] = tok
            S.op("sp", lambda e, jc=jc: e.dma_start(out=scr["xn_sp"][jc], in_=PB1[:, jc * NT:(jc + 1) * NT]),
                 waits=[tok], incs=[("sp", 16)])

        bar_p0 = S.barrier()

        def proj_fmaj(wname, src_pool, out_cb, bar, per_jc=None, wwaits=()):
            wbuf = [pxb(0, 2048), pxb(4, 2048)]
            obuf = [pxb(8, 2048), pxb(12, 2048)]
            tok = None
            for ic in range(NJC):
                b = ic % 2
                bi = ic % 2
                tw = S.op("gp", lambda e, b=b, ic=ic: e.dma_start(out=wbuf[b][:], in_=din[wname][ic]),
                          waits=[S.tc.get(("projw", b))] + (list(wwaits) if ic < 2 else []),
                          incs=[("gp", 16)])
                for jc in range(NJC):
                    for tb in range(NTB):
                        last = (jc == NJC - 1 and tb == NTB - 1)
                        w = []
                        if jc == 0 and tb == 0:
                            w = [tw] + [S.t_ps[k] for k in range(4 * bi, 4 * bi + 4)] + (list(bar) if ic == 0 else [])
                        if per_jc is not None and ic == 0 and tb == 0:
                            w = w + [per_jc.get(jc)]
                        tok = S.op("pe", lambda e, b=b, jc=jc, tb=tb, bi=bi:
                                   e.matmul(PS(4 * bi + tb), wbuf[b][:, jc * 128:(jc + 1) * 128],
                                            src_pool[:, jc * NT + tb * 512: jc * NT + (tb + 1) * 512],
                                            start=(jc == 0), stop=(jc == NJC - 1)),
                                   waits=w, incs=[("pe", 1)] if last else [])
                S.tc[("projw", b)] = tok
                etok = S.op("act", lambda e, b=b, bi=bi: e.activation(out=obuf[b][:], in_=big[bi][:, :],
                                                                      func=AF.Copy),
                            waits=[tok, S.tc.get(("projo", b))], incs=[("act", 1)])
                set_ps(list(range(4 * bi, 4 * bi + 4)), etok)
                S.tc[("projo", b)] = out_cb(ic, etok, obuf[b], b)
            return tok

        ROPEC = pxf(16, 2048)
        ROPES = pxf(24, 2048)

        def rope_and_spill(spill_dst):
            rT = pxf(32, 2048)

            def cb(ic, t_evac, band, b):
                bi = ic % 2
                for tb in range(NTB):
                    trot = S.op("pe", lambda e, tb=tb, bi=bi, band=band:
                                e.matmul(PS(4 * bi + tb), permt[:], band[:, tb * 512:(tb + 1) * 512],
                                         start=True, stop=True),
                                waits=[t_evac] if tb == 0 else [],
                                incs=[("pe", 1)] if tb == NTB - 1 else [])
                t_s = S.op("dve", lambda e, bi=bi: e.tensor_tensor(out=rT[:], in0=big[bi][:, :],
                                                                   in1=ROPES[:], op=ALU.mult),
                           waits=[trot, S.tc.get(("ropert",))], incs=[("dve", 1)])
                set_ps(list(range(4 * bi, 4 * bi + 4)), t_s)
                S.op("dve", lambda e, band=band: e.tensor_tensor(out=band[:], in0=band[:],
                                                                 in1=ROPEC[:], op=ALU.mult))
                S.op("dve", lambda e: e.drain())
                t_add = S.op("dve", lambda e, band=band: e.tensor_tensor(out=band[:], in0=band[:],
                                                                         in1=rT[:], op=ALU.add),
                             incs=[("dve", 1)])
                S.tc[("ropert",)] = t_add
                t_sp = S.op("sp", lambda e, band=band, ic=ic: e.dma_start(out=spill_dst[ic], in_=band[:]),
                            waits=[t_add], incs=[("sp", 16)])
                return t_sp
            return cb

        def plain_spill(spill_dst):
            def cb(ic, t_evac, band, b):
                return S.op("sp", lambda e, band=band, ic=ic: e.dma_start(out=spill_dst[ic], in_=band[:]),
                            waits=[t_evac], incs=[("sp", 16)])
            return cb

        def vtok_spill(spill_dst, stage_kb=40):
            stage = pxb(stage_kb, 2048)

            def cb(ic, t_evac, band, b):
                bi = ic % 2
                bb = BIGB[bi]
                for tc_ in range(NTC):
                    ttr = S.op("pe", lambda e, tc_=tc_, bb=bb, band=band:
                               e.transpose(bb[:, tc_ * 128:(tc_ + 1) * 128],
                                           band[:, tc_ * 128:(tc_ + 1) * 128], identt[:]),
                               waits=[t_evac] if tc_ == 0 else [],
                               incs=[("pe", 1)] if tc_ == NTC - 1 else [])
                etok = S.op("dve", lambda e, bb=bb: e.tensor_copy(out=stage[:], in_=bb[:, 0:2048]),
                            waits=[ttr, S.tc.get(("vstage",))], incs=[("dve", 1)])
                set_ps(list(range(4 * bi, 4 * bi + 4)), etok)
                stok = S.op("sp", lambda e, ic=ic:
                            e.dma_start(out=spill_dst[ic].rearrange("t p c -> p t c"),
                                        in_=stage[:].rearrange("p (t c) -> p t c", t=NTC)),
                            waits=[etok], incs=[("sp", 16)])
                S.tc[("vstage",)] = stok
                return stok
            return cb

        t_ropec = S.op("sp", lambda e: e.dma_start(out=ROPEC[:], in_=din["ropec"]),
                       waits=[t_rr1, t_rr2], incs=[("sp", 16)])
        t_ropes = S.op("sp", lambda e: e.dma_start(out=ROPES[:], in_=din["ropes"]),
                       waits=[t_rr2], incs=[("sp", 16)])
        px_free = [t_stats]
        rope_free = [t_xn[NJC - 1]]
        proj_fmaj("wq1t", PB1, rope_and_spill(scr["q1_sp"]),
                  [t_ropec, t_ropes, t_rr1, t_rr2] + rope_free, per_jc=t_xn, wwaits=px_free)
        proj_fmaj("wk1t", PB1, rope_and_spill(scr["k1_sp"]), [])
        proj_fmaj("wv1t", PB1, vtok_spill(scr["v1_sp"]), [])
        bar_p1a = S.barrier()

        aqh = [pxb(0, 2048), pxb(4, 2048)]
        akh = [pxb(8, 2048), pxb(12, 2048)]
        avh = [pxb(16, 2048), pxb(20, 2048)]
        as4 = [pxb(24, 512), pxb(25, 512), pxb(26, 512), pxb(27, 512)]
        azf = [pxf(28, 512), pxf(30, 512), pxf(32, 512), pxf(34, 512)]
        aqsq = pxb(36, 2048)
        arm1s = [pxf(44, 512), pxf(40, 512)]
        negb16 = pxf(42, 16)
        nmax8 = pxf(43, 8)

        def attention_head(kind, h, sid, q_sp, k_sp, v_sp, outF, bar):
            b = sid
            w0 = list(bar)
            sv_bk = 4 + 2 * sid
            dn_bk = 5 + 2 * sid
            tq = S.op("sp", lambda e, b=b, h=h: e.dma_start(out=aqh[b][:], in_=q_sp[h]),
                      waits=w0 + [S.tc.get(("aq", b))], incs=[("sp", 16)])
            tk = S.op("gp", lambda e, b=b, h=h: e.dma_start(out=akh[b][:], in_=k_sp[h]),
                      waits=w0 + [S.tc.get(("ak", b))], incs=[("gp", 16)])
            tv = S.op("act", lambda e, b=b, h=h:
                      e.dma_start(out=avh[b][:].rearrange("p (t c) -> p t c", t=NTC),
                                  in_=v_sp[h].rearrange("t p c -> p t c")),
                      waits=w0 + [S.tc.get(("av", b))], incs=[("sc", 16)])
            yield

            t_negb = None
            if kind == "sfmx":
                tsq = S.op("dve", lambda e, b=b: e.tensor_tensor(out=aqsq[:], in0=akh[b][:], in1=akh[b][:],
                                                                 op=ALU.mult),
                           waits=[tk, S.tc.get(("aqsq",)), S.tc.get(("aqsqpe",))] + w0,
                           incs=[("dve", 1)])
                for qtr in range(NTB):
                    nbk = sv_bk + qtr % 2
                    tm = S.op("pe", lambda e, qtr=qtr, nbk=nbk: e.matmul(
                              PS(nbk), ones128[:], aqsq[:, qtr * 512:(qtr + 1) * 512],
                              start=True, stop=True),
                              waits=[tsq, S.t_ps[nbk]], incs=[("pe", 1)])
                    tr = S.op("dve", lambda e, qtr=qtr, nbk=nbk: e.reduce_max(
                              out=nmax8[:, qtr:qtr + 1],
                              in_=PS(nbk), axis=mybir.AxisListType.X),
                              waits=[tm], incs=[("dve", 1)])
                    set_ps([nbk], tr)
                S.tc[("nsq",)] = tr
                yield
                tsq2 = S.op("dve", lambda e, b=b: e.tensor_tensor(out=aqsq[:], in0=aqh[b][:], in1=aqh[b][:],
                                                                  op=ALU.mult),
                            waits=[tq, tm], incs=[("dve", 1)])
                for qtr in range(NTB):
                    nbk = sv_bk + qtr % 2
                    tm = S.op("pe", lambda e, qtr=qtr, nbk=nbk: e.matmul(
                              PS(nbk), ones128[:], aqsq[:, qtr * 512:(qtr + 1) * 512],
                              start=True, stop=True),
                              waits=[tsq2, S.t_ps[nbk]], incs=[("pe", 1)])
                    tr = S.op("dve", lambda e, qtr=qtr, nbk=nbk: e.reduce_max(
                              out=nmax8[:, 4 + qtr:4 + qtr + 1],
                              in_=PS(nbk), axis=mybir.AxisListType.X),
                              waits=[tm], incs=[("dve", 1)])
                    set_ps([nbk], tr)
                S.tc[("aqsqpe",)] = tm
                yield
                S.op("dve", lambda e: e.drain())
                S.op("dve", lambda e: e.tensor_tensor(out=nmax8[:, 0:1], in0=nmax8[:, 0:1],
                                                      in1=nmax8[:, 1:2], op=ALU.max))
                S.op("dve", lambda e: e.tensor_tensor(out=nmax8[:, 2:3], in0=nmax8[:, 2:3],
                                                      in1=nmax8[:, 3:4], op=ALU.max))
                S.op("dve", lambda e: e.tensor_tensor(out=nmax8[:, 4:5], in0=nmax8[:, 4:5],
                                                      in1=nmax8[:, 5:6], op=ALU.max))
                S.op("dve", lambda e: e.tensor_tensor(out=nmax8[:, 6:7], in0=nmax8[:, 6:7],
                                                      in1=nmax8[:, 7:8], op=ALU.max))
                S.op("dve", lambda e: e.drain())
                S.op("dve", lambda e: e.tensor_tensor(out=nmax8[:, 0:1], in0=nmax8[:, 0:1],
                                                      in1=nmax8[:, 2:3], op=ALU.max))
                S.op("dve", lambda e: e.tensor_tensor(out=nmax8[:, 4:5], in0=nmax8[:, 4:5],
                                                      in1=nmax8[:, 6:7], op=ALU.max))
                S.op("dve", lambda e: e.drain())
                tpd = S.op("dve", lambda e: e.tensor_tensor(out=nmax8[:, 0:1], in0=nmax8[:, 0:1],
                                                            in1=nmax8[:, 4:5], op=ALU.mult),
                           incs=[("dve", 1)])
                tln = S.op("act", lambda e: e.activation(out=nmax8[:, 0:1], in_=nmax8[:, 0:1], func=AF.Ln),
                           waits=[tpd], incs=[("act", 1)])
                tex = S.op("act", lambda e: e.activation(out=nmax8[:, 0:1], in_=nmax8[:, 0:1],
                                                         func=AF.Exp, scale=0.5),
                           waits=[tln], incs=[("act", 1)])
                t_negb = S.op("dve", lambda e, h=h: e.tensor_scalar_mul(
                              out=negb16[:, h:h + 1], in0=nmax8[:, 0:1], scalar1=-RSQ_HD),
                              waits=[tex], incs=[("dve", 1)])
                yield

            tout = None
            for tqb in range(NTB):
                ntile = 4 * tqb + 4
                t_w = {}
                tsv = None
                for t_ in range(ntile + 2):
                    if t_ >= 2:
                        tt_ = t_ - 2
                        sl2 = 2 * sid + tt_ % 2
                        S.op("pe", lambda e, sl2=sl2, tt_=tt_, b=b, sv_bk=sv_bk,
                             st=(tt_ == 0), en=(tt_ == ntile - 1):
                             e.matmul(PS(sv_bk), avh[b][:, tt_ * 128:(tt_ + 1) * 128],
                                      as4[sl2][:], start=st, stop=en),
                             waits=[t_w[tt_], tv] + ([S.t_ps[sv_bk]] if tt_ == 0 else []))
                        tsv = S.op("pe", lambda e, sl2=sl2, tt_=tt_, dn_bk=dn_bk,
                                   st=(tt_ == 0), en=(tt_ == ntile - 1):
                                   e.matmul(PS(dn_bk), ones128[:], as4[sl2][:], start=st, stop=en),
                                   waits=[S.t_ps[dn_bk]] if tt_ == 0 else [],
                                   incs=[("pe", 1)])
                        S.tc[("as4", sl2)] = tsv
                    if t_ < ntile:
                        sl = 2 * sid + t_ % 2
                        diag = t_ >= 4 * tqb
                        dc = (t_ - 4 * tqb) * 128 if diag else 0
                        tqk = S.op("pe", lambda e, sl=sl, t_=t_, tqb=tqb, b=b:
                                   e.matmul(big[0][:, sl * 512:(sl + 1) * 512],
                                            akh[b][:, t_ * 128:(t_ + 1) * 128],
                                            aqh[b][:, tqb * 512:(tqb + 1) * 512],
                                            start=True, stop=True),
                                   waits=[tq, tk, S.t_ps[sl]], incs=[("pe", 1)])
                        tms = None
                        if dc > 0:
                            tms = S.op("dve", lambda e, sl=sl, dc=dc: e.memset(as4[sl][:, 0:dc], 0.0),
                                       waits=[S.tc.get(("as4", sl))], incs=[("dve", 1)])
                        if kind == "poly":
                            ts2 = S.op("act", lambda e, sl=sl, dc=dc: e.square(
                                       out=azf[sl][:, dc:512],
                                       in_=big[0][:, sl * 512 + dc:(sl + 1) * 512]),
                                       waits=[tqk, S.tc.get(("az", sl))], incs=[("act", 1)])
                            set_ps([sl], ts2)
                            tw4 = S.op("gp", lambda e, sl=sl, dc=dc: e.tensor_tensor(
                                       out=as4[sl][:, dc:512], in0=azf[sl][:, dc:512],
                                       in1=azf[sl][:, dc:512], op=ALU.mult),
                                       waits=[ts2, S.tc.get(("as4", sl))], incs=[("gpe", 1)])
                            S.tc[("az", sl)] = tw4
                        else:
                            tw4 = S.op("act", lambda e, sl=sl, h=h, dc=dc: e.activation(
                                       out=as4[sl][:, dc:512],
                                       in_=big[0][:, sl * 512 + dc:(sl + 1) * 512],
                                       func=AF.Exp, scale=RSQ_HD, bias=negb16[:, h:h + 1]),
                                       waits=[tqk, t_negb, S.tc.get(("as4", sl))],
                                       incs=[("act", 1)])
                            set_ps([sl], tw4)
                            S.tc[("az", sl)] = tw4
                        t_ready = tw4
                        if diag:
                            d = t_ - 4 * tqb
                            bend = min(dc + 128, 512)
                            t_ready = S.op("dve", lambda e, sl=sl, d=d, dc=dc, bend=bend:
                                           e.tensor_tensor(
                                           out=as4[sl][:, dc:bend], in0=as4[sl][:, dc:bend],
                                           in1=maskt[:, d * 512 + dc: d * 512 + bend], op=ALU.mult),
                                           waits=[tw4], incs=[("dve", 1)])
                        t_w[t_] = [t_ready, tms] if tms is not None else t_ready
                    if t_ % 2 == 1:
                        yield
                arm1 = arm1s[sid]
                if kind == "poly":
                    tdm = S.op("dve", lambda e, dn_bk=dn_bk, arm1=arm1: e.tensor_scalar_max(
                               out=arm1[:], in0=PS(dn_bk), scalar1=1e-6),
                               waits=[tsv, S.tc.get(("arm", sid))], incs=[("dve", 1)])
                    set_ps([dn_bk], tdm)
                    tln = S.op("act", lambda e, arm1=arm1: e.activation(out=arm1[:], in_=arm1[:], func=AF.Ln),
                               waits=[tdm, S.tc.get(("arm", sid))], incs=[("act", 1)])
                else:
                    tln = S.op("act", lambda e, dn_bk=dn_bk, arm1=arm1: e.activation(
                               out=arm1[:], in_=PS(dn_bk), func=AF.Ln),
                               waits=[tsv, S.tc.get(("arm", sid))], incs=[("act", 1)])
                    set_ps([dn_bk], tln)
                trc = S.op("act", lambda e, arm1=arm1: e.activation(out=arm1[:], in_=arm1[:], func=AF.Exp,
                                                         scale=-1.0),
                           waits=[tln], incs=[("act", 1)])
                tout = S.op("dve", lambda e, h=h, tqb=tqb, sv_bk=sv_bk, arm1=arm1: e.tensor_tensor(
                            out=outF[:, h * NT + tqb * 512: h * NT + (tqb + 1) * 512],
                            in0=PS(sv_bk), in1=arm1[:], op=ALU.mult),
                            waits=[trc], incs=[("dve", 1)])
                S.tc[("arm", sid)] = tout
                set_ps([sv_bk], tout)
                yield
            S.tc[("aq", b)] = tqk
            S.tc[("ak", b)] = tqk
            S.tc[("av", b)] = tsv

        def proj_out(wname, srcF, band_cb, bar, fixed_bi=None):
            wbuf = [pxb(0, 2048), pxb(4, 2048)]
            obuf = [pxf(8, 2048), pxf(16, 2048)]
            for ic in range(NJC):
                b = ic % 2
                bi = (ic % 2) if fixed_bi is None else fixed_bi
                tw = S.op("gp", lambda e, b=b, ic=ic: e.dma_start(out=wbuf[b][:], in_=din[wname][ic]),
                          waits=[S.tc.get(("projw", b))] + (list(bar) if ic < 2 else []), incs=[("gp", 16)])
                for cin in range(NJC):
                    for tb in range(NTB):
                        last = (cin == NJC - 1 and tb == NTB - 1)
                        tok = S.op("pe", lambda e, b=b, cin=cin, tb=tb, bi=bi:
                                   e.matmul(PS(4 * bi + tb), wbuf[b][:, cin * 128:(cin + 1) * 128],
                                            srcF[:, cin * NT + tb * 512: cin * NT + (tb + 1) * 512],
                                            start=(cin == 0), stop=(cin == NJC - 1)),
                                   waits=([tw] + [S.t_ps[k] for k in range(4 * bi, 4 * bi + 4)] +
                                          (list(bar) if ic == 0 else []))
                                   if (cin == 0 and tb == 0) else [],
                                   incs=[("pe", 1)] if last else [])
                S.tc[("projw", b)] = tok
                etok = S.op("dve", lambda e, b=b, bi=bi: e.tensor_copy(out=obuf[b][:], in_=big[bi][:, :]),
                            waits=[tok, S.tc.get(("projo", b))], incs=[("dve", 1)])
                set_ps(list(range(4 * bi, 4 * bi + 4)), etok)
                S.tc[("projo", b)] = band_cb(ic, etok, obuf[b], b)

        trl = S.op("sp", lambda e: e.dma_start(out=R(2), in_=scr["rows_sp"][2:3, :]),
                   waits=bar_p1a, incs=[("sp", 16)])
        RSHREP = pxf(16, 2048)
        t_rsh = replicate_row(R(2), RSHREP, [trl] + bar_p1a, bi=1)
        h1t = [pxf(24, 2048), pxf(36, 2048)]
        th = None
        for jc in range(NJC):
            hb = h1t[jc % 2]
            tt = S.op("dve", lambda e, jc=jc, hb=hb: e.tensor_tensor(
                      out=hb[:], in0=PB1[:, jc * NT:(jc + 1) * NT], in1=RSHREP[:], op=ALU.mult),
                      waits=[t_rsh, S.tc.get(("h1t", jc % 2))], incs=[("dve", 1)])
            th = S.op("act", lambda e, jc=jc, hb=hb: e.activation(
                      out=PB2[:, jc * NT:(jc + 1) * NT], in_=hb[:], func=AF.Copy,
                      scale=ln1s[:, jc:jc + 1]),
                      waits=[tt], incs=[("act", 1)])
            S.tc[("h1t", jc % 2)] = th
        bar_h1 = S.barrier()

        proj_fmaj("wq2t", PB2, plain_spill(scr["q2_sp"]), bar_h1)
        proj_fmaj("wk2t", PB2, plain_spill(scr["k2_sp"]), [])
        proj_fmaj("wv2t", PB2, vtok_spill(scr["v2_sp"], stage_kb=32), [])
        bar_p2a = S.barrier()

        for h in range(H):
            bar_h = bar_p2a if h == 0 else []
            ga = attention_head("poly", h, 0, scr["q1_sp"], scr["k1_sp"], scr["v1_sp"], PB1, bar_h)
            gb = attention_head("sfmx", h, 1, scr["q2_sp"], scr["k2_sp"], scr["v2_sp"], PB2, bar_h)
            a_done = b_done = False
            while not (a_done and b_done):
                if not a_done:
                    try:
                        next(ga)
                    except StopIteration:
                        a_done = True
                if not b_done:
                    try:
                        next(gb)
                    except StopIteration:
                        b_done = True
        bar_attn = S.barrier()

        def attno_cb(ic, t_evac, band, b):
            return S.op("sp", lambda e, band=band, ic=ic: e.dma_start(out=scr["attno_sp"][ic], in_=band[:]),
                        waits=[t_evac], incs=[("sp", 16)])
        proj_out("wo1t", PB1, attno_cb, bar_attn)
        bar_p1c = S.barrier()

        TANH1 = pxf(32, 2048)

        def attn2o_cb(ic, t_evac, band, b):
            bb = pxb(24 + 4 * b, 2048)
            tc_ = S.op("dve", lambda e, bb=bb, band=band: e.tensor_copy(out=bb[:], in_=band[:]),
                       waits=[t_evac, S.tc.get(("a2bb", b))], incs=[("dve", 1)])
            tok = None
            for tb in range(NTB):
                tok = S.op("pe", lambda e, tb=tb, ic=ic, bb=bb:
                           e.matmul(PS(4 + tb, rows=16), at[:, ic * 16:(ic + 1) * 16],
                                    bb[:, tb * 512:(tb + 1) * 512],
                                    start=(ic == 0), stop=(ic == NJC - 1)),
                           waits=([tc_, t_at] + ([S.t_ps[k] for k in (4, 5, 6, 7)] if ic == 0 else []))
                           if tb == 0 else [],
                           incs=[("pe", 1)] if tb == NTB - 1 else [])
            S.tc[("a2bb", b)] = tok
            return tok
        proj_out("wo2t", PB2, attn2o_cb, bar_p1c, fixed_bi=0)
        t_moda = S.barrier()
        t_tanh = S.op("act", lambda e: e.activation(out=TANH1[0:16, :], in_=big[1][0:16, :], func=AF.Tanh),
                      waits=t_moda, incs=[("act", 1)])
        set_ps([4, 5, 6, 7], t_tanh)
        bar_p2c = S.barrier()

        trl2 = S.op("sp", lambda e: e.dma_start(out=R(2), in_=scr["rows_sp"][2:3, :]),
                    waits=bar_p2c, incs=[("sp", 16)])
        RSHREP2 = pxf(16, 2048)
        t_rsh2 = replicate_row(R(2), RSHREP2, [trl2] + bar_p2c, bi=1)
        xnb = [pxb(0, 2048), pxb(4, 2048)]
        h2t = [pxf(8, 2048), pxf(36, 2048)]
        t_h2 = {}
        th2 = None
        for jc in range(NJC):
            b = jc % 2
            hb = h2t[b]
            tl = S.op("sp", lambda e, b=b, jc=jc: e.dma_start(out=xnb[b][:], in_=scr["xn_sp"][jc]),
                      waits=[S.tc.get(("xnb", b))] + (bar_p2c if jc < 2 else []), incs=[("sp", 16)])
            tt = S.op("dve", lambda e, b=b, hb=hb: e.tensor_tensor(out=hb[:], in0=xnb[b][:], in1=RSHREP2[:],
                                                                   op=ALU.mult),
                      waits=[tl, t_rsh2, S.tc.get(("h2t", b))], incs=[("dve", 1)])
            th2 = S.op("act", lambda e, jc=jc, hb=hb: e.activation(
                       out=PB1[:, jc * NT:(jc + 1) * NT], in_=hb[:], func=AF.Copy,
                       scale=ln2s[:, jc:jc + 1]),
                       waits=[tt], incs=[("act", 1)])
            S.tc[("xnb", b)] = th2
            S.tc[("h2t", b)] = th2
            t_h2[jc] = th2
        tok = None
        for jc in range(NJC):
            for tb in range(NTB):
                last = (jc == NJC - 1 and tb == NTB - 1)
                tok = S.op("pe", lambda e, jc=jc, tb=tb:
                           e.matmul(PS(tb, rows=16), at[:, jc * 16:(jc + 1) * 16],
                                    PB1[:, jc * NT + tb * 512: jc * NT + (tb + 1) * 512],
                                    start=(jc == 0), stop=(jc == NJC - 1)),
                           waits=([t_h2[jc]] + ([S.t_ps[k] for k in range(4)] if jc == 0 else []))
                           if tb == 0 else [],
                           incs=[("pe", 1)] if last else [])
        t_h2a = tok
        H2A = pxf(24, 2048)
        M2 = pxb(40, 2048)
        te = S.op("dve", lambda e: e.tensor_copy(out=H2A[0:16, :], in_=big[0][0:16, :]),
                  waits=[t_h2a], incs=[("dve", 1)])
        set_ps([0, 1, 2, 3], te)
        S.op("dve", lambda e: e.drain())
        S.op("dve", lambda e: e.tensor_tensor(out=H2A[0:16, :], in0=TANH1[0:16, :], in1=H2A[0:16, :],
                                              op=ALU.mult), waits=[t_tanh])
        S.op("dve", lambda e: e.drain())
        t_m2 = S.op("dve", lambda e: e.tensor_scalar_mul(out=M2[0:16, :], in0=H2A[0:16, :], scalar1=0.1),
                    incs=[("dve", 1)])
        S.op("dve", lambda e: e.drain())

        gub = [pxb(0, 8192), pxb(16, 8192)]
        sgt = pxf(32, 512)
        ui = 0
        tgu = None
        for fb in range(NFB):
            wb = fb % 2
            winb = PB2[:, wb * 8192: wb * 8192 + 8192]
            wupb = PB2[:, 16384 + wb * 8192: 16384 + wb * 8192 + 8192]
            _wprev = S.tc.get(("winb", wb)) or []
            _wprev = _wprev if isinstance(_wprev, list) else [_wprev]
            twi = S.op("gp", lambda e, winb=winb, fb=fb: e.dma_start(out=winb, in_=din["wint"][fb]),
                       waits=_wprev + (bar_p2c if fb < 2 else []), incs=[("gp", 16)])
            twu = S.op("gp", lambda e, wupb=wupb, fb=fb: e.dma_start(out=wupb, in_=din["wupt"][fb]),
                       incs=[("gp", 16)])
            tbb = S.op("gp", lambda e, wb=wb, fb=fb:
                       e.dma_start(out=bband[:, wb * FB:(wb + 1) * FB], in_=din["b_t"][:, fb * FB:(fb + 1) * FB]),
                       incs=[("gp", 16)])
            for fc in range(4):
                for tb in range(NTB):
                    gbk = ui % 4
                    ubk = 4 + ui % 4
                    for jc in range(NJC):
                        S.op("pe", lambda e, jc=jc, fc=fc, tb=tb, gbk=gbk, winb=winb:
                             e.matmul(PS(gbk), winb[:, jc * FB + fc * 128: jc * FB + (fc + 1) * 128],
                                      PB1[:, jc * NT + tb * 512: jc * NT + (tb + 1) * 512],
                                      start=(jc == 0), stop=False),
                             waits=[twi, twu, tbb, t_m2, t_h2[NJC - 1], S.t_ps[gbk]] if jc == 0 else [])
                        S.op("pe", lambda e, jc=jc, fc=fc, tb=tb, ubk=ubk, wupb=wupb:
                             e.matmul(PS(ubk), wupb[:, jc * FB + fc * 128: jc * FB + (fc + 1) * 128],
                                      PB1[:, jc * NT + tb * 512: jc * NT + (tb + 1) * 512],
                                      start=(jc == 0), stop=(jc == NJC - 1)),
                             waits=[S.t_ps[ubk]] if jc == 0 else [])
                    t_gmm = S.op("pe", lambda e, fc=fc, tb=tb, gbk=gbk, wb=wb:
                                 e.matmul(PS(gbk), bband[:, wb * FB + fc * 128: wb * FB + (fc + 1) * 128],
                                          M2[0:16, tb * 512:(tb + 1) * 512], start=False, stop=True),
                                 incs=[("pe", 1)])
                    tsg = S.op("act", lambda e, gbk=gbk: e.activation(out=sgt[:], in_=PS(gbk), func=AF.Silu),
                               waits=[t_gmm, S.tc.get(("sgt",))], incs=[("act", 1)])
                    _gprev = S.tc.get(("gub", wb)) or []
                    _gprev = _gprev if isinstance(_gprev, list) else [_gprev]
                    tgu = S.op("dve", lambda e, fc=fc, tb=tb, ubk=ubk, wb=wb:
                               e.tensor_tensor(out=gub[wb][:, fc * 2048 + tb * 512: fc * 2048 + (tb + 1) * 512],
                                               in0=sgt[:], in1=PS(ubk), op=ALU.mult),
                               waits=[tsg] + _gprev, incs=[("dve", 1)])
                    S.tc[("sgt",)] = tgu
                    set_ps([gbk, ubk], tgu)
                    ui += 1
            tspills = []
            for tbx in range(NTB):
                tgsp = S.op("sp", lambda e, wb=wb, fb=fb, tbx=tbx:
                            e.dma_start(out=scr["gu_sp"][tbx, fb * 4:(fb + 1) * 4].rearrange("f p t -> p f t"),
                                        in_=gub[wb].rearrange("p (f r) -> p f r", f=4)[:, :, tbx * 512:(tbx + 1) * 512]),
                            waits=[tgu] if tbx == 0 else [], incs=[("sp", 16)])
                tspills.append(tgsp)
            S.tc[("winb", wb)] = tspills
            S.tc[("gub", wb)] = tspills
        bar_p3a = S.barrier()

        trs = S.op("sp", lambda e: e.dma_start(out=R(2), in_=scr["rows_sp"][0:1, :]),
                   waits=bar_p3a, incs=[("sp", 16)])
        trm = S.op("sp", lambda e: e.dma_start(out=R(1), in_=scr["rows_sp"][1:2, :]),
                   incs=[("sp", 16)])
        ta1 = S.op("dve", lambda e: e.tensor_scalar_add(out=R(2), in0=R(2), scalar1=1.0),
                   waits=[trs] + bar_p3a, incs=[("dve", 1)])
        AREP = pxf(16, 2048)
        MREP = pxf(24, 2048)
        t_ar = replicate_row(R(2), AREP, [ta1] + bar_p3a, bi=1)
        t_mr = replicate_row(R(1), MREP, [trm], bi=1)

        xtile = [pxf(32, 512), pxf(34, 512)]
        aotile = [pxf(36, 512), pxf(38, 512)]
        e1 = pxf(40, 512)
        e2 = pxf(42, 512)
        for icb in range(NICB):
            twd = {}
            for s in range(2):
                twd[s] = S.op("gp", lambda e, icb=icb, s=s: e.dma_start(
                              out=PB2[:, s * 16384:(s + 1) * 16384],
                              in_=din["wdnt"][icb][:, s * 16384:(s + 1) * 16384]),
                              waits=[S.tc.get(("wdh", s))] + (bar_p3a if icb == 0 else []),
                              incs=[("gp", 16)])
            for tqb in range(NTB):
                tgul = {}
                for s in range(2):
                    tgul[s] = S.op("gp", lambda e, tqb=tqb, s=s: e.dma_start(
                                   out=PB1[:, s * 16384:(s + 1) * 16384].rearrange("p (f t) -> p f t", f=32),
                                   in_=scr["gu_sp"][tqb, 32 * s:32 * s + 32].rearrange("f p t -> p f t")),
                                   waits=[S.tc.get(("guh", s))] + (bar_p3a if icb == 0 and tqb == 0 else []),
                                   incs=[("gp", 16)])
                tok = None
                t_mlps = {}
                for s in range(2):
                    for ici in range(4):
                        bk = ici % 4
                        for fc in range(32 * s, 32 * s + 32):
                            last = (fc == NFC - 1)
                            tok = S.op("pe", lambda e, fc=fc, ici=ici, bk=bk, last=last:
                                       e.matmul(PS(bk), PB2[:, fc * ICB + ici * 128: fc * ICB + (ici + 1) * 128],
                                                PB1[:, fc * 512:(fc + 1) * 512], start=(fc == 0), stop=last),
                                       waits=([twd[s], tgul[s]] + ([S.t_ps[bk]] if s == 0 else []))
                                       if fc == 32 * s else [],
                                       incs=[("pe", 1)] if (last or fc == 32 * s + 31) else [])
                        if s == 1:
                            t_mlps[ici] = tok
                    S.tc[("guh", s)] = tok
                    S.tc[("wdh", s)] = tok
                for ici in range(4):
                    ic = icb * 4 + ici
                    bk = ici % 4
                    t_mlp = t_mlps[ici]
                    eb = ici % 2
                    tlx = S.op("act", lambda e, ic=ic, tqb=tqb, eb=eb:
                               e.dma_start(out=xtile[eb][:],
                                           in_=xT[ic * 128:(ic + 1) * 128, tqb * 512:(tqb + 1) * 512]),
                               waits=[S.tc.get(("xts", eb))] + (bar_p3a if icb == 0 and tqb == 0 else []),
                               incs=[("sc", 16)])
                    tla = S.op("act", lambda e, ic=ic, tqb=tqb, eb=eb:
                               e.dma_start(out=aotile[eb][:],
                                           in_=scr["attno_sp"][ic][:, tqb * 512:(tqb + 1) * 512]),
                               incs=[("sc", 16)])
                    te1 = S.op("dve", lambda e, eb=eb, tqb=tqb: e.tensor_tensor(
                               out=e1[:], in0=xtile[eb][:], in1=AREP[:, tqb * 512:(tqb + 1) * 512],
                               op=ALU.mult),
                               waits=[tlx, t_ar, t_mr, S.tc.get(("e1",))], incs=[("dve", 1)])
                    S.op("dve", lambda e: e.drain())
                    S.op("dve", lambda e, tqb=tqb: e.tensor_tensor(
                         out=e1[:], in0=e1[:], in1=MREP[:, tqb * 512:(tqb + 1) * 512], op=ALU.subtract))
                    te2 = S.op("dve", lambda e, eb=eb, bk=bk: e.tensor_tensor(
                               out=e2[:], in0=PS(bk), in1=aotile[eb][:], op=ALU.add),
                               waits=[t_mlp, tla, S.tc.get(("e2",))], incs=[("dve", 1)])
                    set_ps([bk], te2)
                    S.op("dve", lambda e: e.drain())
                    teo = S.op("dve", lambda e: e.tensor_tensor(out=e1[:], in0=e1[:], in1=e2[:], op=ALU.add),
                               incs=[("dve", 1)])
                    S.op("dve", lambda e: e.drain())
                    S.tc[("e2",)] = teo
                    tst = S.op("sp", lambda e, ic=ic, tqb=tqb: e.dma_start(
                               out=outT[ic * 128:(ic + 1) * 128, tqb * 512:(tqb + 1) * 512], in_=e1[:]),
                               waits=[teo], incs=[("st", 16)])
                    S.tc[("e1",)] = tst
                    S.tc[("xts", eb)] = tst

        S.op("sp", None, waits=[(s, S.count[s]) for s in QSEMS["st"] if S.count[s] > 0])
        S.emit(block, sems)
    return nc


def kernel(**inputs):
    g, xts = host_prep(inputs)
    nc = build()
    in_maps = []
    for c in range(8):
        m = dict(g)
        m["xt"] = xts[c]
        in_maps.append(m)
    res = run_bass_kernel_spmd(nc, in_maps, core_ids=list(range(8)))
    outs = [np.ascontiguousarray(np.asarray(r["outt"], np.float32).T) for r in res.results]
    return np.stack(outs)
